# version 08
# baseline (speedup 1.0000x reference)
import numpy as np
import ml_dtypes
from contextlib import ExitStack

import concourse.bass as bass
import concourse.mybir as mybir
import concourse.bass_utils as _bass_utils
from concourse.bass_utils import run_bass_kernel_spmd

F32 = mybir.dt.float32
BF16 = mybir.dt.bfloat16
U32 = mybir.dt.uint32
AF = mybir.ActivationFunctionType
ALU = mybir.AluOpType
AX = mybir.AxisListType

NCORE = 8
T_CORE = 2048
NBLK = 2
TBLK = 1024
NTC = 8
D = 2048
NDC = 16
DH = 64
EPS = 1e-5
INTER = 4096
NIC = 32
NE = 8
HM = 1408
NHC = 11
HS = 2816
NSC = 22
NEG_BIG = -3.0e38

OUT_STAGE = "x2"


class Sched:
    ENGINES = ("sync", "tensor", "vector", "scalar", "gpsimd")

    def __init__(self, nc):
        self.nc = nc
        self.ops = []
        self.counts = {}
        self.paranoid = False
        self.all_sems = []
        self.own_sems = {}

    def sem(self, name):
        h = self.nc.alloc_semaphore(name=name)
        self.counts[name] = 0
        self.all_sems.append(h)
        return h

    def op(self, engine, fn, waits=(), inc=None, inc_amt=1):
        if self.paranoid:
            waits = list(waits) + [(s, self.counts[s.name])
                                   for s in self.all_sems]
        ws = [(s, v) for (s, v) in waits if v > 0]
        own = self.own_sems.get(engine)
        own_val = self.counts[own.name] if own is not None else 0
        self.ops.append((engine, fn, ws, inc, inc_amt, own, own_val))
        if inc is not None:
            self.counts[inc.name] += inc_amt
            return self.counts[inc.name]
        return None

    def bump(self, sem, amt=1):
        self.counts[sem.name] += amt
        return self.counts[sem.name]

    def val(self, sem):
        return self.counts[sem.name]

    def emit(self, block):
        stats = {e: 0 for e in self.ENGINES}

        def body(eng_name):
            def run(eng):
                seen = {}
                for engine, fn, waits, inc, inc_amt, own, own_val in self.ops:
                    if engine != eng_name:
                        continue
                    if own is not None and seen.get(own.name, -1) < own_val:
                        eng.wait_ge(own, own_val)
                        seen[own.name] = own_val
                    for (s, v) in waits:
                        if seen.get(s.name, -1) < v:
                            eng.wait_ge(s, v)
                            seen[s.name] = v
                    inst = fn(eng)
                    stats[eng_name] += 1
                    if inc is not None:
                        inst.then_inc(inc, inc_amt)
            return run

        block.sync(body("sync"))
        block.tensor(body("tensor"))
        block.vector(body("vector"))
        block.scalar(body("scalar"))
        block.gpsimd(body("gpsimd"))
        return stats


class Mem:
    def __init__(self, nc):
        self.nc = nc
        self.guards = {}
        self.serial = 0

    def _uniq(self, name):
        self.serial += 1
        return f"{name}__{self.serial}"

    def sbuf(self, name, shape, dtype, side="left"):
        g = self.nc.sbuf_tensor(self._uniq(name), shape, dtype, side=side)
        h = g.__enter__()
        self.guards[name] = g
        return h

    def psum(self, name, shape, dtype):
        g = self.nc.psum_tensor(self._uniq(name), shape, dtype)
        h = g.__enter__()
        self.guards[name] = g
        return h

    def free(self, *names):
        for name in names:
            self.guards.pop(name).__exit__(None, None, None)


def build_program(out_stage=OUT_STAGE, paranoid=()):
    nc = bass.Bass("TRN2", target_bir_lowering=False)

    def din(name, shape, dtype=F32):
        return nc.dram_tensor(name, shape, dtype, kind="ExternalInput").ap()

    x_d = din("x", [T_CORE, D])
    wqkvT_d = din("wqkvT", [D, 3 * D])
    woT_d = din("woT", [D, D])
    wupT_d = din("wupT", [D, 2 * INTER])
    wdT_d = din("wdT", [INTER, D])
    gatew_d = din("gatew", [D, NE])
    mgu_d = din("mgu", [NE * D, 2 * HM])
    mdown_d = din("mdown", [NE * HM, D])
    swgu_d = din("swgu", [D, 2 * HS])
    swd_d = din("swd", [HS, D])
    cosc_d = din("cosc", [T_CORE, DH])
    sintc_d = din("sintc", [T_CORE, DH])
    identb_d = din("identb", [128, 128], BF16)
    maskt_d = din("maskt", [128, 128])
    onesf_d = din("onesf", [128, 128])

    y_d = nc.dram_tensor("y", [T_CORE, D], F32, kind="ExternalOutput").ap()
    spill_d = nc.dram_tensor("spill", [T_CORE, D], F32, kind="Internal").ap()

    S = Sched(nc)
    M = Mem(nc)

    s_w = S.sem("s_w")
    s_c = S.sem("s_c")
    s_cs = S.sem("s_cs")
    s_x = [S.sem("s_x0"), S.sem("s_x1")]
    s_o = [S.sem(f"s_o{i}") for i in range(4)]
    s_pe = S.sem("s_pe")
    s_dve = S.sem("s_dve")
    s_act = S.sem("s_act")
    s_pool = S.sem("s_pool")
    s_join = S.sem("s_join")
    S.own_sems = {"vector": s_dve, "scalar": s_act, "gpsimd": s_pool}

    identb = M.sbuf("c_identb", [128, 128], BF16)
    maskt = M.sbuf("c_maskt", [128, 128], F32)
    onesf = M.sbuf("c_onesf", [128, 128], F32)
    attnw = M.sbuf("attnw", [128, NDC], F32)
    ffnw = M.sbuf("ffnw", [128, NDC], F32)
    gwf = M.sbuf("gwf", [128, NDC * NE], F32)
    gwb = M.sbuf("gwb", [128, NDC * NE], BF16)
    xsl = [M.sbuf(f"xsl{i}", [128, 512], F32) for i in range(2)]
    outr = [M.sbuf(f"outr{i}", [128, 512], F32) for i in range(4)]
    junk = M.sbuf("junk", [128, 512], F32)
    stn = ["sum1", "sq1", "m1v", "var1", "lnA", "r1", "lnV", "dd", "v2", "lnB",
           "lnT", "s12", "b12", "zb1", "sqs2", "r2b", "sum3", "sq3", "m3v",
           "var3", "ln3A", "s3", "b3", "tstat"]
    ST = {n: M.sbuf("st_" + n, [128, NTC], F32) for n in stn}
    sq4 = M.sbuf("sq4", [128, NTC * 4], F32)
    s34 = M.sbuf("s34", [128, 128], F32)
    nmax = [M.sbuf(f"nmax{i}", [128, 1], F32) for i in range(2)]
    bias8 = [M.sbuf(f"bias8{i}", [128, 1], F32) for i in range(2)]
    epsT = M.sbuf("epsT", [128, 1], F32)

    xsl_free = [0, 0]
    outr_free = [0, 0, 0, 0]

    JOIN_ENGINES = ("tensor", "vector", "scalar", "gpsimd")

    def join():
        own = {"tensor": s_pe, "vector": s_dve, "scalar": s_act,
               "gpsimd": s_pool}
        for e in JOIN_ENGINES:
            S.op(e, lambda eng, s=own[e], v=S.val(own[e]): eng.wait_ge(s, v))
            S.op(e, lambda eng: eng.sem_inc(s_join, 1))
            S.bump(s_join, 1)
        b = S.val(s_join)
        for e in JOIN_ENGINES:
            S.op(e, lambda eng, b=b: eng.wait_ge(s_join, b))
        return b

    def dma(engine, out, in_, sem, waits=()):
        return S.op(engine, lambda e, o=out, i=in_: e.dma_start(out=o, in_=i),
                    waits=waits, inc=sem, inc_amt=16)

    S.op("vector", lambda e: e.memset(epsT[:], EPS), inc=s_dve)
    for (dst, src) in ((identb, identb_d), (maskt, maskt_d),
                       (onesf, onesf_d)):
        dma("sync", dst[:], src, s_c)
    attnw_in_d = din("attnw_in", [128, NDC])
    ffnw_in_d = din("ffnw_in", [128, NDC])
    dma("sync", attnw[:], attnw_in_d, s_c)
    dma("sync", ffnw[:], ffnw_in_d, s_c)
    dma("sync", gwf[:].rearrange("p (c e) -> p c e", e=NE),
        gatew_d.rearrange("(c p) e -> p c e", p=128), s_c)
    v_consts = S.val(s_c)
    S.op("gpsimd", lambda e: e.tensor_copy(out=gwb[:], in_=gwf[:]),
         waits=[(s_c, v_consts)], inc=s_pool)

    class WStream:
        def __init__(self, tag, src_ap, src_rows, ncols_og, dgrp, scale=None,
                     nbuf=2, side="left"):
            self.tag = tag
            self.src = src_ap
            self.nrc = src_rows // 128
            self.ncols = ncols_og
            self.dgrp = dgrp
            self.scale = scale
            self.s_f = [S.sem(f"sw_{tag}_{M.serial}_{i}") for i in range(2)]
            self.fst = [M.sbuf(f"wf_{tag}{i}", [128, dgrp * ncols_og], F32,
                               side=side) for i in range(2)]
            self.bf = [M.sbuf(f"wb_{tag}{i}", [128, self.nrc * ncols_og], BF16,
                              side=side) for i in range(nbuf)]
            self.nbuf = nbuf
            self.fst_free = [S.val(s_act), S.val(s_act)]
            self.bf_free = [S.val(s_pe)] * nbuf
            self.bf_ready = [0] * nbuf
            self.n = 0
            self.first_wait = [(s_join, S.val(s_join)),
                               (s_pool, S.val(s_pool)),
                               (s_pe, S.val(s_pe)),
                               (s_dve, S.val(s_dve))]

        def load(self, col0, ncols=None, nrc=None, rc0=0):
            ncols = ncols or self.ncols
            nrc = nrc or self.nrc
            bi = self.n % self.nbuf
            ngrp = (nrc + self.dgrp - 1) // self.dgrp
            conv_waits = [(s_pe, self.bf_free[bi])]
            pv = 0
            for g in range(ngrp):
                r0 = g * self.dgrp
                nr = min(self.dgrp, nrc - r0)
                fi = (self.n * ngrp + g) % 2
                src = self.src.rearrange("(c p) o -> p c o", p=128)[
                    :, rc0 + r0: rc0 + r0 + nr, col0:col0 + ncols]
                dst = self.fst[fi][:, 0:nr * ncols].rearrange(
                    "p (c o) -> p c o", o=ncols)
                w = dma("sync", dst, src, self.s_f[fi],
                        waits=self.first_wait + [(s_act, self.fst_free[fi])])
                self.first_wait = []
                fsl = self.fst[fi][:, 0:nr * ncols]
                bsl = self.bf[bi][:, r0 * ncols:(r0 + nr) * ncols]
                pv = S.op("scalar", lambda e, o=bsl, i=fsl:
                          e.copy(out=o, in_=i),
                          waits=[(self.s_f[fi], w)] + conv_waits, inc=s_act)
                conv_waits = []
                self.fst_free[fi] = pv
            self.bf_ready[bi] = pv
            self.n += 1
            return bi

        def consumed(self, bi, pe_val):
            self.bf_free[bi] = pe_val

        def done(self):
            M.free(*(f"wb_{self.tag}{i}" for i in reversed(range(self.nbuf))))
            M.free(*(f"wf_{self.tag}{i}" for i in reversed(range(2))))

    def mm_group(psum_ap, pairs, extra_first_waits=()):
        n = len(pairs)
        v = None
        for i, (lt, rh) in enumerate(pairs):
            v = S.op("tensor",
                     lambda e, o=psum_ap, lt=lt, rh=rh, st=(i == 0),
                     sp=(i == n - 1): e.matmul(o, lt, rh, start=st, stop=sp),
                     waits=list(extra_first_waits) if i == 0 else [],
                     inc=s_pe if i == n - 1 else None)
        return v

    def transpose_to(ps, src_sl, dst_sl, waits=(), scale=None):
        pv = S.op("tensor", lambda e, ps=ps, s=src_sl:
                  e.transpose(ps[:], s, identb[:]), waits=waits, inc=s_pe)
        if scale is None:
            dv = S.op("vector", lambda e, ps=ps, d=dst_sl:
                      e.tensor_copy(out=d, in_=ps[:]), waits=[(s_pe, pv)],
                      inc=s_dve)
        else:
            dv = S.op("vector", lambda e, ps=ps, d=dst_sl, sc=scale:
                      e.tensor_scalar(out=d, in0=ps[:], scalar1=sc,
                                      scalar2=None, op0=ALU.mult),
                      waits=[(s_pe, pv)], inc=s_dve)
        return dv

    def ln_chain(tc, sum_b, sq_b, want_rms2):
        def sl(t):
            return t[:, tc:tc + 1]

        dv = S.op("vector", lambda e: e.tensor_scalar(
            out=sl(ST["m1v"]), in0=sl(sum_b), scalar1=1.0 / D, scalar2=None,
            op0=ALU.mult), waits=[(s_act, S.val(s_act))], inc=s_dve)
        dv = S.op("vector", lambda e: e.tensor_mul(
            sl(ST["tstat"]), sl(ST["m1v"]), sl(ST["m1v"])),
            waits=[(s_dve, dv)], inc=s_dve)
        dv = S.op("vector", lambda e: e.scalar_tensor_tensor(
            out=sl(ST["var1"]), in0=sl(sq_b), scalar=1.0 / D,
            in1=sl(ST["tstat"]), op0=ALU.mult, op1=ALU.subtract),
            waits=[(s_dve, dv)], inc=s_dve)
        av = S.op("scalar", lambda e: e.activation(
            out=sl(ST["lnA"]), in_=sl(ST["var1"]), func=AF.Ln, bias=epsT[:]),
            waits=[(s_dve, dv)], inc=s_act)
        av = S.op("scalar", lambda e: e.activation(
            out=sl(ST["r1"]), in_=sl(ST["lnA"]), func=AF.Exp, scale=-0.5),
            waits=[(s_act, av)], inc=s_act)
        if want_rms2:
            av = S.op("scalar", lambda e: e.activation(
                out=sl(ST["lnV"]), in_=sl(ST["var1"]), func=AF.Ln),
                waits=[(s_act, av)], inc=s_act)
            dv = S.op("vector", lambda e: e.tensor_sub(
                sl(ST["dd"]), sl(ST["lnV"]), sl(ST["lnA"])),
                waits=[(s_act, av)], inc=s_dve)
            av = S.op("scalar", lambda e: e.activation(
                out=sl(ST["v2"]), in_=sl(ST["dd"]), func=AF.Exp),
                waits=[(s_dve, dv)], inc=s_act)
            av = S.op("scalar", lambda e: e.activation(
                out=sl(ST["lnB"]), in_=sl(ST["v2"]), func=AF.Ln, bias=epsT[:]),
                waits=[(s_act, av)], inc=s_act)
            dv = S.op("vector", lambda e: e.tensor_add(
                sl(ST["lnT"]), sl(ST["lnA"]), sl(ST["lnB"])),
                waits=[(s_act, av)], inc=s_dve)
            av = S.op("scalar", lambda e: e.activation(
                out=sl(ST["s12"]), in_=sl(ST["lnT"]), func=AF.Exp, scale=-0.5),
                waits=[(s_dve, dv)], inc=s_act)
            dv = S.op("vector", lambda e: e.scalar_tensor_tensor(
                out=sl(ST["b12"]), in0=sl(ST["m1v"]), scalar=-1.0,
                in1=sl(ST["s12"]), op0=ALU.mult, op1=ALU.mult),
                waits=[(s_act, av)], inc=s_dve)
        dv = S.op("vector", lambda e: e.scalar_tensor_tensor(
            out=sl(ST["zb1"]), in0=sl(ST["m1v"]), scalar=-1.0,
            in1=sl(ST["r1"]), op0=ALU.mult, op1=ALU.mult),
            waits=[(s_act, S.val(s_act)), (s_dve, dv)], inc=s_dve)
        return dv

    block_guard = nc.Block()
    block = block_guard.__enter__()

    for blk in range(NBLK):
        row0 = blk * TBLK
        S.paranoid = 's1' in paranoid
        def stage_s1():
            h1T = M.sbuf("h1T", [128, NDC * TBLK], BF16, side="left")
            scratch = M.sbuf("scratch", [128, D], F32, side="left")
            h1tile = M.sbuf("h1tile", [128, D], BF16, side="left")
            xr = [M.sbuf(f"xr{i}", [128, D], F32, side="left") for i in range(2)]
            xr_free = [[], []]
            ps_tr = [M.psum(f"ps_tr{i}", [128, 128], BF16) for i in range(2)]
            tr_free = [0, 0]

            h1_busy = 0
            for tc in range(NTC):
                ri = tc % 2
                xv = dma("scalar", xr[ri][:],
                         x_d[row0 + tc * 128: row0 + (tc + 1) * 128, :], s_x[ri],
                         waits=list(xr_free[ri]) + [(s_join, S.val(s_join))])
                av = S.op("scalar", lambda e, ri=ri, tc=tc: e.activation(
                    out=scratch[:], in_=xr[ri][:], func=AF.Copy,
                    accum_out=ST["sum1"][:, tc:tc + 1]),
                    waits=[(s_x[ri], xv)], inc=s_act)
                av = S.op("scalar", lambda e, ri=ri, tc=tc: e.activation(
                    out=scratch[:], in_=xr[ri][:], func=AF.Square,
                    accum_out=ST["sq1"][:, tc:tc + 1]),
                    waits=[(s_act, av)], inc=s_act)
                dv = ln_chain(tc, ST["sum1"], ST["sq1"], want_rms2=True)
                pl = S.op("gpsimd", lambda e, ri=ri, tc=tc: e.tensor_scalar(
                    out=h1tile[:], in0=xr[ri][:],
                    scalar1=ST["s12"][:, tc:tc + 1],
                    scalar2=ST["b12"][:, tc:tc + 1],
                    op0=ALU.mult, op1=ALU.add),
                    waits=[(s_dve, dv), (s_pool, S.val(s_pool)),
                           (s_pe, h1_busy)], inc=s_pool)
                xr_free[ri] = [(s_act, av), (s_pool, pl)]
                for c in range(NDC):
                    ti = (tc * NDC + c) % 2
                    dv2 = transpose_to(
                        ps_tr[ti], h1tile[:, c * 128:(c + 1) * 128],
                        h1T[:, c * TBLK + tc * 128: c * TBLK + (tc + 1) * 128],
                        waits=[(s_pool, pl), (s_dve, tr_free[ti])],
                        scale=attnw[:, c:c + 1])
                    tr_free[ti] = dv2
                h1_busy = S.val(s_pe)
            M.free("xr1", "xr0", "h1tile", "scratch")
            M.free("ps_tr1", "ps_tr0")
            join()
            return h1T
        h1T = stage_s1()
        S.paranoid = 's2' in paranoid
        def stage_s2():
            cosT = M.sbuf("cosT", [128, NTC * DH], F32, side="left")
            sintT = M.sbuf("sintT", [128, NTC * DH], F32, side="left")
            dma("sync", cosT[:].rearrange("p (c j) -> p c j", j=DH),
                cosc_d.rearrange("(c p) j -> p c j", p=128)[:, blk * NTC:(blk + 1) * NTC, :],
                s_cs, waits=[(s_join, S.val(s_join))])
            dma("sync", sintT[:].rearrange("p (c j) -> p c j", j=DH),
                sintc_d.rearrange("(c p) j -> p c j", p=128)[:, blk * NTC:(blk + 1) * NTC, :],
                s_cs)
            cs_ready = S.val(s_cs)
            q_tok = M.sbuf("q_tok", [128, NTC * D], BF16, side="right")
            k_tok = M.sbuf("k_tok", [128, NTC * D], BF16, side="right")
            vtok = M.sbuf("vtok", [128, NTC * 2112], BF16, side="right")
            rt1 = M.sbuf("rt1", [128, 512], F32, side="left")
            rt2 = M.sbuf("rt2", [128, 512], F32, side="left")
            ws_qkv = WStream("qkv", wqkvT_d, D, 512, 2)
            ps_mm = [M.psum(f"ps_mm{i}", [128, 512], F32) for i in range(2)]
            mm_free = [0, 0]

            S.op("gpsimd", lambda e: e.memset(
                vtok[:].rearrange("p (g x) -> p g x", x=66)[:, :, 64:65], 1.0),
                waits=[(s_join, S.val(s_join))], inc=s_pool)
            S.op("gpsimd", lambda e: e.memset(
                vtok[:].rearrange("p (g x) -> p g x", x=66)[:, :, 65:66], 0.0),
                inc=s_pool)
            vinit = S.val(s_pool)

            for og in range(12):
                bi = ws_qkv.load(og * 512)
                wready = ws_qkv.bf_ready[bi]
                last_pe = None
                for tc in range(NTC):
                    pi = (og * NTC + tc) % 2
                    pairs = [(h1T[:, c * TBLK + tc * 128: c * TBLK + (tc + 1) * 128],
                              ws_qkv.bf[bi][:, c * 512:(c + 1) * 512])
                             for c in range(NDC)]
                    pv = mm_group(ps_mm[pi][:], pairs,
                                  extra_first_waits=[(s_act, wready),
                                                     (s_dve, mm_free[pi]),
                                                     (s_join, S.val(s_join))])
                    last_pe = pv
                    if og < 8:
                        dstbuf = q_tok if og < 4 else k_tok
                        col0 = (og % 4) * 512
                        gtc = tc
                        ps3 = ps_mm[pi][:].rearrange("p (h j) -> p h j", j=DH)
                        t13 = rt1[:].rearrange("p (h j) -> p h j", j=DH)
                        t23 = rt2[:].rearrange("p (h j) -> p h j", j=DH)
                        cossl = cosT[:, gtc * DH:(gtc + 1) * DH]
                        sinsl = sintT[:, gtc * DH:(gtc + 1) * DH]
                        dv = S.op("vector", lambda e, ps3=ps3, t13=t13, c=cossl:
                                  e.tensor_tensor(
                                      out=t13, in0=ps3,
                                      in1=c[:, None, :].broadcast_to([128, 8, DH]),
                                      op=ALU.mult),
                                  waits=[(s_pe, pv), (s_dve, S.val(s_dve)),
                                         (s_cs, cs_ready)],
                                  inc=s_dve)
                        dv = S.op("vector", lambda e, ps3=ps3, t23=t23, s=sinsl:
                                  e.tensor_tensor(
                                      out=t23[:, :, 0:32], in0=ps3[:, :, 32:64],
                                      in1=s[:, None, 0:32].broadcast_to([128, 8, 32]),
                                      op=ALU.mult), inc=s_dve)
                        dv = S.op("vector", lambda e, ps3=ps3, t23=t23, s=sinsl:
                                  e.tensor_tensor(
                                      out=t23[:, :, 32:64], in0=ps3[:, :, 0:32],
                                      in1=s[:, None, 32:64].broadcast_to([128, 8, 32]),
                                      op=ALU.mult), inc=s_dve)
                        mm_free[pi] = dv
                        dv = S.op("vector", lambda e, d=dstbuf, tc=tc, col0=col0:
                                  e.tensor_add(
                                      d[:, tc * D + col0: tc * D + col0 + 512],
                                      rt1[:], rt2[:]),
                                  waits=[(s_dve, dv)], inc=s_dve)
                    else:
                        h0 = (og - 8) * 8
                        dst = vtok[:, tc * 2112 + h0 * 66: tc * 2112 + (h0 + 8) * 66] \
                            .rearrange("p (h x) -> p h x", x=66)[:, :, 0:64]
                        src = ps_mm[pi][:].rearrange("p (h j) -> p h j", j=DH)
                        dv = S.op("vector", lambda e, d=dst, s=src:
                                  e.tensor_copy(out=d, in_=s),
                                  waits=[(s_pe, pv), (s_pool, vinit)], inc=s_dve)
                        mm_free[pi] = dv
                ws_qkv.consumed(bi, last_pe)
            ws_qkv.done()
            M.free("rt2", "rt1", "sintT", "cosT")
            M.free("h1T")
            M.free("ps_mm1", "ps_mm0")
            join()
            return q_tok, k_tok, vtok
        q_tok, k_tok, vtok = stage_s2()
        S.paranoid = 's3' in paranoid
        def stage_s3():
            ctxT = M.sbuf("ctxT", [128, NDC * TBLK], BF16, side="left")
            qTp = [M.sbuf(f"qTp{i}", [128, TBLK], BF16, side="left") for i in range(2)]
            kTp = [M.sbuf(f"kTp{i}", [128, TBLK], BF16, side="left") for i in range(2)]
            praw = [M.sbuf(f"praw{i}", [128, TBLK], F32, side="left") for i in range(2)]
            probs = [M.sbuf(f"probs{i}", [128, TBLK], BF16, side="left") for i in range(2)]
            probsT = [M.sbuf(f"probsT{i}", [128, 4608], BF16, side="left")
                      for i in range(2)]
            sums = [M.sbuf(f"sums{i}", [128, 1], F32, side="left") for i in range(2)]
            rmax = [M.sbuf(f"rmax{i}", [128, 1], F32, side="left") for i in range(2)]
            rcp = [M.sbuf(f"rcp{i}", [128, 1], F32, side="left") for i in range(2)]
            ps_tr = [M.psum(f"ps_tr{i}", [128, 128], BF16) for i in range(2)]
            ps_sc = [M.psum(f"ps_sc{i}", [128, TBLK], F32) for i in range(2)]
            ps_ctx = M.psum("ps_ctx", [64, TBLK], F32)
            tr_free = [0, 0]
            probs_free = [0, 0]
            probsT_free = [0, 0]
            sc_free = [0, 0]
            praw_free = [0, 0]
            ctx_free = 0

            def offp(c):
                return c * TBLK - 128 * (c * (c - 1) // 2)

            for pc in range(NDC):
                pb = pc % 2
                w0 = [(s_join, S.val(s_join))]
                for tc in range(NTC):
                    for (srcb, dstb) in ((q_tok, qTp[pb]), (k_tok, kTp[pb])):
                        ti = tc % 2
                        dv2 = transpose_to(
                            ps_tr[ti],
                            srcb[:, tc * D + pc * 128: tc * D + (pc + 1) * 128],
                            dstb[:, tc * 128:(tc + 1) * 128],
                            waits=w0 + [(s_dve, tr_free[ti])])
                        w0 = []
                        tr_free[ti] = dv2
                qk_ready = S.val(s_dve)
                trdone_h = [0, 0]
                for qi in range(NTC):
                    ext = (qi + 1) * 128
                    for hh in range(2):
                        h = 2 * pc + hh
                        b0 = hh * 64
                        par = hh
                        lt = qTp[pb][b0:b0 + 64, qi * 128:(qi + 1) * 128]
                        segs = [(o, min(ext, o + 512)) for o in range(0, ext, 512)]
                        pv = None
                        for (a, b) in segs:
                            pv = S.op("tensor", lambda e, lt=lt, a=a, b=b, pb=pb,
                                      b0=b0, par=par:
                                      e.matmul(ps_sc[par][:, a:b], lt,
                                               kTp[pb][b0:b0 + 64, a:b],
                                               start=True, stop=True),
                                      waits=([(s_dve, qk_ready),
                                              (s_act, sc_free[par])]
                                             if a == 0 else []),
                                      inc=s_pe if b == ext else None)
                        dv = S.op("vector", lambda e, qi=qi, ext=ext, par=par:
                                  e.tensor_tensor(
                                      out=ps_sc[par][:, qi * 128:ext],
                                      in0=ps_sc[par][:, qi * 128:ext],
                                      in1=maskt[:], op=ALU.add),
                                  waits=[(s_pe, pv)], inc=s_dve)
                        dv = S.op("vector", lambda e, qi=qi, ext=ext, par=par:
                                  e.reduce_max(nmax[par][:],
                                               ps_sc[par][:, qi * 128:ext],
                                               axis=AX.X),
                                  waits=[(s_dve, dv),
                                         (s_act, sc_free[par])],
                                  inc=s_dve)
                        if qi == 0:
                            dv = S.op("vector", lambda e, par=par:
                                      e.tensor_copy(out=rmax[par][:],
                                                    in_=nmax[par][:]),
                                      waits=[(s_dve, dv)], inc=s_dve)
                        else:
                            dv = S.op("vector", lambda e, par=par:
                                      e.tensor_max(rmax[par][:], rmax[par][:],
                                                   nmax[par][:]),
                                      waits=[(s_dve, dv)], inc=s_dve)
                        dv = S.op("vector", lambda e, par=par: e.tensor_scalar(
                            out=bias8[par][:], in0=rmax[par][:], scalar1=-0.125,
                            scalar2=None, op0=ALU.mult), waits=[(s_dve, dv)],
                            inc=s_dve)
                        av = S.op("scalar", lambda e, ext=ext, par=par:
                                  e.activation(
                                      out=praw[par][:, 0:ext],
                                      in_=ps_sc[par][:, 0:ext],
                                      func=AF.Exp, bias=bias8[par][:],
                                      scale=0.125, accum_out=sums[par][:]),
                                  waits=[(s_dve, dv), (s_act, praw_free[par])],
                                  inc=s_act)
                        sc_free[par] = av
                        dv = S.op("vector", lambda e, par=par: e.reciprocal(
                            rcp[par][:], sums[par][:]), waits=[(s_act, av)],
                            inc=s_dve)
                        av = S.op("scalar", lambda e, ext=ext, par=par:
                                  e.activation(
                                      out=probs[par][:, 0:ext],
                                      in_=praw[par][:, 0:ext], func=AF.Copy,
                                      scale=rcp[par][:]),
                                  waits=[(s_dve, dv), (s_pe, probs_free[par])],
                                  inc=s_act)
                        praw_free[par] = av
                        for c in range(qi + 1):
                            ti = c % 2
                            pv2 = S.op("tensor", lambda e, par=par, c=c, ti=ti:
                                       e.transpose(
                                           ps_tr[ti][:],
                                           probs[par][:, c * 128:(c + 1) * 128],
                                           identb[:]),
                                       waits=[(s_act, av), (s_dve, tr_free[ti])],
                                       inc=s_pe)
                            probs_free[par] = pv2
                            dst = probsT[par][:, offp(c) + (qi - c) * 128:
                                              offp(c) + (qi - c) * 128 + 128]
                            dv2 = S.op("vector", lambda e, d=dst, ti=ti:
                                       e.tensor_copy(out=d, in_=ps_tr[ti][:]),
                                       waits=[(s_pe, pv2),
                                              (s_pe, probsT_free[par])],
                                       inc=s_dve)
                            tr_free[ti] = dv2
                        trdone_h[hh] = S.val(s_dve)
                for hh in range(2):
                    h = 2 * pc + hh
                    b0 = hh * 64
                    hb = hh
                    pv = None
                    for cc in range(NTC):
                        lt = vtok[:, cc * 2112 + h * 66: cc * 2112 + h * 66 + 64]
                        lo = cc * 128
                        segs = [(lo, 512), (512, TBLK)] if lo < 512 else [(lo, TBLK)]
                        for (a, b) in segs:
                            is_start = (cc == 0)
                            is_stop = (cc == 3 and b == 512) or \
                                      (cc == 7 and b == TBLK)
                            pv = S.op("tensor", lambda e, lt=lt, a=a, b=b, cc=cc,
                                      st=is_start, sp=is_stop, hb=hb:
                                      e.matmul(ps_ctx[:, a:b], lt,
                                               probsT[hb][:, offp(cc) + a - cc * 128:
                                                          offp(cc) + b - cc * 128],
                                               start=st, stop=sp),
                                      waits=([(s_dve, trdone_h[hh]),
                                              (s_dve, ctx_free)]
                                             if (cc == 0 and a == 0) else []),
                                      inc=s_pe)
                    probsT_free[hb] = pv
                    dv = S.op("vector", lambda e, pc=pc, b0=b0: e.tensor_copy(
                        out=ctxT[b0:b0 + 64, pc * TBLK:(pc + 1) * TBLK],
                        in_=ps_ctx[:]), waits=[(s_pe, pv)], inc=s_dve)
                    ctx_free = dv
            M.free("ps_ctx", "ps_sc1", "ps_sc0", "ps_tr1", "ps_tr0")
            M.free("rcp1", "rcp0", "rmax1", "rmax0", "sums1", "sums0", "probsT1", "probsT0",
                   "probs1", "probs0", "praw1", "praw0",
                   "kTp1", "kTp0", "qTp1", "qTp0")
            M.free("vtok", "k_tok", "q_tok")
            join()
            return ctxT
        ctxT = stage_s3()
        S.paranoid = 's4' in paranoid
        def stage_s4():
            RES = M.sbuf("RES", [128, NTC * D], F32, side="right")
            ztmp = [M.sbuf(f"ztmp{i}", [128, 512], F32, side="left") for i in range(2)]
            ws_wo = WStream("wo", woT_d, D, 512, 2)
            ps_mm = [M.psum(f"ps_mm{i}", [128, 512], F32) for i in range(2)]
            mm_free = [0, 0]
            for og in range(4):
                bi = ws_wo.load(og * 512)
                wready = ws_wo.bf_ready[bi]
                last_pe = None
                for tc in range(NTC):
                    pi = (og * NTC + tc) % 2
                    pairs = [(ctxT[:, c * TBLK + tc * 128: c * TBLK + (tc + 1) * 128],
                              ws_wo.bf[bi][:, c * 512:(c + 1) * 512])
                             for c in range(NDC)]
                    pv = mm_group(ps_mm[pi][:], pairs,
                                  extra_first_waits=[(s_act, wready),
                                                     (s_dve, mm_free[pi]),
                                                     (s_join, S.val(s_join))])
                    last_pe = pv
                    xi = tc % 2
                    xv = dma("scalar", xsl[xi][:],
                             x_d[row0 + tc * 128: row0 + (tc + 1) * 128,
                                 og * 512:(og + 1) * 512], s_x[xi],
                             waits=[(s_dve, xsl_free[xi])])
                    zi = tc % 2
                    zv = S.op("vector", lambda e, xi=xi, zi=zi, tc=tc:
                              e.tensor_scalar(
                                  out=ztmp[zi][:], in0=xsl[xi][:],
                                  scalar1=ST["r1"][:, tc:tc + 1],
                                  scalar2=ST["zb1"][:, tc:tc + 1],
                                  op0=ALU.mult, op1=ALU.add),
                              waits=[(s_x[xi], xv)], inc=s_dve)
                    res_sl = RES[:, tc * D + og * 512: tc * D + (og + 1) * 512]
                    dv = S.op("vector", lambda e, pi=pi, zi=zi, r=res_sl:
                              e.tensor_add(r, ps_mm[pi][:], ztmp[zi][:]),
                              waits=[(s_pe, pv)], inc=s_dve)
                    mm_free[pi] = dv
                    xsl_free[xi] = dv
                    av = S.op("scalar", lambda e, r=res_sl, tc=tc, og=og:
                              e.activation(
                                  out=junk[:], in_=r, func=AF.Square,
                                  accum_out=sq4[:, tc * 4 + og: tc * 4 + og + 1]),
                              waits=[(s_dve, dv), (s_act, S.val(s_act))], inc=s_act)
                ws_wo.consumed(bi, last_pe)
            ws_wo.done()
            M.free("ztmp1", "ztmp0")
            M.free("ctxT")
            h2T = M.sbuf("h2T", [128, NDC * TBLK], BF16, side="right")
            h2tile = M.sbuf("h2tile", [128, D], BF16, side="left")
            ps_trh = [M.psum(f"ps_trh{i}", [128, 128], BF16) for i in range(2)]
            trh_free = [0, 0]
            sq4done = S.val(s_act)
            h2_busy = 0
            for tc in range(NTC):
                dv = S.op("vector", lambda e, tc=tc: e.reduce_sum(
                    ST["sqs2"][:, tc:tc + 1], sq4[:, tc * 4:(tc + 1) * 4],
                    axis=AX.X), waits=[(s_act, sq4done)], inc=s_dve)
                av = S.op("scalar", lambda e, tc=tc: e.activation(
                    out=ST["lnV"][:, tc:tc + 1], in_=ST["sqs2"][:, tc:tc + 1],
                    func=AF.Ln, scale=1.0 / D, bias=epsT[:]), waits=[(s_dve, dv)],
                    inc=s_act)
                av = S.op("scalar", lambda e, tc=tc: e.activation(
                    out=ST["r2b"][:, tc:tc + 1], in_=ST["lnV"][:, tc:tc + 1],
                    func=AF.Exp, scale=-0.5), waits=[(s_act, av)], inc=s_act)
                av = S.op("scalar", lambda e, tc=tc: e.activation(
                    out=h2tile[:], in_=RES[:, tc * D:(tc + 1) * D], func=AF.Copy,
                    scale=ST["r2b"][:, tc:tc + 1]),
                    waits=[(s_act, av), (s_pe, h2_busy)], inc=s_act)
                for c in range(NDC):
                    ti = (tc * NDC + c) % 2
                    dv2 = transpose_to(
                        ps_trh[ti], h2tile[:, c * 128:(c + 1) * 128],
                        h2T[:, c * TBLK + tc * 128: c * TBLK + (tc + 1) * 128],
                        waits=[(s_act, av), (s_dve, trh_free[ti])],
                        scale=ffnw[:, c:c + 1])
                    trh_free[ti] = dv2
                h2_busy = S.val(s_pe)
            M.free("h2tile")
            M.free("ps_trh1", "ps_trh0", "ps_mm1", "ps_mm0")
            join()
            return RES, h2T
        RES, h2T = stage_s4()
        if out_stage == "z2":
            for tc in range(NTC):
                dma("sync", y_d[row0 + tc * 128: row0 + (tc + 1) * 128, :],
                    RES[:, tc * D:(tc + 1) * D], s_o[tc % 4],
                    waits=[(s_join, S.val(s_join))])
            for i in range(4):
                S.op("sync", lambda e, s=s_o[i], v=S.val(s_o[i]):
                     e.wait_ge(s, v))
            S.op("sync", lambda e: e.sem_inc(s_join, 4))
            S.bump(s_join, 4)
            M.free("h2T", "RES")
            continue
        S.paranoid = 's5' in paranoid
        def stage_s5():
            svT = M.sbuf("svT", [128, NIC * TBLK], BF16, side="left")
            sgt = [M.sbuf(f"sgt{i}", [128, 512], BF16, side="left") for i in range(2)]
            ws_up = WStream("up", wupT_d, D, 256, 4)
            ps_g = [M.psum(f"ps_g{i}", [128, 512], F32) for i in range(2)]
            ps_u = [M.psum(f"ps_u{i}", [128, 512], F32) for i in range(2)]
            g_free = [0, 0]
            u_free = [0, 0]
            for og in range(32):
                bi = ws_up.load(og * 256)
                wready = ws_up.bf_ready[bi]
                last_pe = None
                for ts in range(2):
                    pi = (og * 2 + ts) % 2
                    gpairs = [(ws_up.bf[bi][:, c * 256: c * 256 + 128],
                               h2T[:, c * TBLK + ts * 512:
                                   c * TBLK + (ts + 1) * 512])
                              for c in range(NDC)]
                    upairs = [(ws_up.bf[bi][:, c * 256 + 128: c * 256 + 256],
                               h2T[:, c * TBLK + ts * 512:
                                   c * TBLK + (ts + 1) * 512])
                              for c in range(NDC)]
                    pvg = mm_group(ps_g[pi][:], gpairs,
                                   extra_first_waits=[(s_act, wready),
                                                      (s_act, g_free[pi]),
                                                      (s_join, S.val(s_join))])
                    pvu = mm_group(ps_u[pi][:], upairs,
                                   extra_first_waits=[(s_dve, u_free[pi])])
                    last_pe = pvu
                    av = S.op("scalar", lambda e, pi=pi: e.activation(
                        out=sgt[pi][:], in_=ps_g[pi][:], func=AF.Silu),
                        waits=[(s_pe, pvg), (s_dve, u_free[pi])], inc=s_act)
                    g_free[pi] = av
                    dst = svT[:, og * TBLK + ts * 512: og * TBLK + (ts + 1) * 512]
                    dv = S.op("vector", lambda e, pi=pi, d=dst:
                              e.tensor_tensor(out=d, in0=ps_u[pi][:],
                                              in1=sgt[pi][:], op=ALU.mult),
                              waits=[(s_pe, pvu), (s_act, av)], inc=s_dve)
                    u_free[pi] = dv
                ws_up.consumed(bi, last_pe)
            ws_up.done()
            M.free("sgt1", "sgt0")
            M.free("ps_u1", "ps_u0", "ps_g1", "ps_g0")
            M.free("h2T")
            join()
            return svT
        svT = stage_s5()
        S.paranoid = 's6' in paranoid
        def stage_s6():
            ws_wd = WStream("wd", wdT_d, INTER, 256, 8)
            ps_mm = [M.psum(f"ps_mm{i}", [128, 256], F32) for i in range(2)]
            mm_free = [0, 0]
            for oh in range(8):
                bi = ws_wd.load(oh * 256)
                wready = ws_wd.bf_ready[bi]
                last_pe = None
                for tc in range(NTC):
                    pi = (oh * NTC + tc) % 2
                    pairs = [(svT[:, i * TBLK + tc * 128: i * TBLK + (tc + 1) * 128],
                              ws_wd.bf[bi][:, i * 256:(i + 1) * 256])
                             for i in range(NIC)]
                    pv = mm_group(ps_mm[pi][:], pairs,
                                  extra_first_waits=[(s_act, wready),
                                                     (s_dve, mm_free[pi]),
                                                     (s_join, S.val(s_join))])
                    last_pe = pv
                    xi = tc % 2
                    xv = dma("scalar", xsl[xi][:, 0:256],
                             x_d[row0 + tc * 128: row0 + (tc + 1) * 128,
                                 oh * 256:(oh + 1) * 256], s_x[xi],
                             waits=[(s_dve, xsl_free[xi])])
                    oi = (oh * NTC + tc) % 4
                    res_sl = RES[:, tc * D + oh * 256: tc * D + (oh + 1) * 256]
                    dv = S.op("vector", lambda e, pi=pi, oi=oi, r=res_sl:
                              e.tensor_add(outr[oi][:, 0:256], ps_mm[pi][:], r),
                              waits=[(s_pe, pv), (s_o[oi], outr_free[oi])],
                              inc=s_dve)
                    mm_free[pi] = dv
                    dv = S.op("vector", lambda e, oi=oi, xi=xi: e.tensor_add(
                        outr[oi][:, 0:256], outr[oi][:, 0:256], xsl[xi][:, 0:256]),
                        waits=[(s_x[xi], xv), (s_dve, dv)], inc=s_dve)
                    xsl_free[xi] = dv
                    av1 = S.op("scalar", lambda e, oi=oi, tc=tc, oh=oh: e.activation(
                        out=junk[:, 0:256], in_=outr[oi][:, 0:256], func=AF.Copy,
                        accum_out=s34[:, tc * 8 + oh: tc * 8 + oh + 1]),
                        waits=[(s_dve, dv), (s_act, S.val(s_act))], inc=s_act)
                    av2 = S.op("scalar", lambda e, oi=oi, tc=tc, oh=oh: e.activation(
                        out=junk[:, 0:256], in_=outr[oi][:, 0:256], func=AF.Square,
                        accum_out=s34[:, 64 + tc * 8 + oh: 64 + tc * 8 + oh + 1]),
                        waits=[(s_act, av1)], inc=s_act)
                    sv = dma("sync", spill_d[row0 + tc * 128: row0 + (tc + 1) * 128,
                                             oh * 256:(oh + 1) * 256],
                             outr[oi][:, 0:256], s_o[oi], waits=[(s_act, av2)])
                    outr_free[oi] = sv
                ws_wd.consumed(bi, last_pe)
            ws_wd.done()
            M.free("svT")
            M.free("ps_mm1", "ps_mm0")
            M.free("RES")
            join()
            return None
        stage_s6()
        if out_stage == "x1":
            for tc in range(NTC):
                for og in range(4):
                    xv = dma("sync", xsl[0][:],
                             spill_d[row0 + tc * 128: row0 + (tc + 1) * 128,
                                     og * 512:(og + 1) * 512], s_c,
                             waits=[(s_o[0], S.val(s_o[0]))])
                    dma("sync", y_d[row0 + tc * 128: row0 + (tc + 1) * 128,
                                    og * 512:(og + 1) * 512], xsl[0][:],
                        s_o[0], waits=[(s_c, xv)])
            continue
        S.paranoid = 's7' in paranoid
        def stage_s7():
            ln3T = M.sbuf("ln3T", [128, NDC * TBLK], BF16, side="right")
            l3tile = M.sbuf("l3tile", [128, D], BF16, side="left")
            xr = [M.sbuf(f"xr{i}", [128, D], F32, side="left") for i in range(2)]
            xr_free = [0, 0]
            ps_tr = [M.psum(f"ps_tr{i}", [128, 128], BF16) for i in range(2)]
            tr_free = [0, 0]
            l3_busy = 0
            spill_done = [(s_o[i], S.val(s_o[i])) for i in range(4)]
            for tc in range(NTC):
                dv = S.op("vector", lambda e, tc=tc: e.reduce_sum(
                    ST["sum3"][:, tc:tc + 1], s34[:, tc * 8:(tc + 1) * 8],
                    axis=AX.X),
                    waits=[(s_act, S.val(s_act)), (s_join, S.val(s_join))],
                    inc=s_dve)
                dv = S.op("vector", lambda e, tc=tc: e.reduce_sum(
                    ST["sq3"][:, tc:tc + 1], s34[:, 64 + tc * 8: 64 + (tc + 1) * 8],
                    axis=AX.X), waits=[(s_dve, dv)], inc=s_dve)
                dv = S.op("vector", lambda e, tc=tc: e.tensor_scalar(
                    out=ST["m3v"][:, tc:tc + 1], in0=ST["sum3"][:, tc:tc + 1],
                    scalar1=1.0 / D, scalar2=None, op0=ALU.mult),
                    waits=[(s_dve, dv)], inc=s_dve)
                dv = S.op("vector", lambda e, tc=tc: e.tensor_mul(
                    ST["tstat"][:, tc:tc + 1], ST["m3v"][:, tc:tc + 1],
                    ST["m3v"][:, tc:tc + 1]), waits=[(s_dve, dv)], inc=s_dve)
                dv = S.op("vector", lambda e, tc=tc: e.scalar_tensor_tensor(
                    out=ST["var3"][:, tc:tc + 1], in0=ST["sq3"][:, tc:tc + 1],
                    scalar=1.0 / D, in1=ST["tstat"][:, tc:tc + 1],
                    op0=ALU.mult, op1=ALU.subtract), waits=[(s_dve, dv)], inc=s_dve)
                av = S.op("scalar", lambda e, tc=tc: e.activation(
                    out=ST["ln3A"][:, tc:tc + 1], in_=ST["var3"][:, tc:tc + 1],
                    func=AF.Ln, bias=epsT[:]), waits=[(s_dve, dv)], inc=s_act)
                av = S.op("scalar", lambda e, tc=tc: e.activation(
                    out=ST["s3"][:, tc:tc + 1], in_=ST["ln3A"][:, tc:tc + 1],
                    func=AF.Exp, scale=-0.5), waits=[(s_act, av)], inc=s_act)
                dv = S.op("vector", lambda e, tc=tc: e.scalar_tensor_tensor(
                    out=ST["b3"][:, tc:tc + 1], in0=ST["m3v"][:, tc:tc + 1],
                    scalar=-1.0, in1=ST["s3"][:, tc:tc + 1], op0=ALU.mult,
                    op1=ALU.mult), waits=[(s_act, av)], inc=s_dve)
                ri = tc % 2
                xv = dma("scalar", xr[ri][:],
                         spill_d[row0 + tc * 128: row0 + (tc + 1) * 128, :],
                         s_x[ri],
                         waits=[(s_pool, xr_free[ri])] + spill_done)
                pl = S.op("gpsimd", lambda e, ri=ri, tc=tc: e.tensor_scalar(
                    out=l3tile[:], in0=xr[ri][:],
                    scalar1=ST["s3"][:, tc:tc + 1],
                    scalar2=ST["b3"][:, tc:tc + 1],
                    op0=ALU.mult, op1=ALU.add),
                    waits=[(s_x[ri], xv), (s_dve, dv), (s_pool, S.val(s_pool)),
                           (s_pe, l3_busy)],
                    inc=s_pool)
                xr_free[ri] = pl
                for c in range(NDC):
                    ti = (tc * NDC + c) % 2
                    dv2 = transpose_to(
                        ps_tr[ti], l3tile[:, c * 128:(c + 1) * 128],
                        ln3T[:, c * TBLK + tc * 128: c * TBLK + (tc + 1) * 128],
                        waits=[(s_pool, pl), (s_dve, tr_free[ti])])
                    tr_free[ti] = dv2
                l3_busy = S.val(s_pe)
            M.free("xr1", "xr0", "l3tile")
            M.free("ps_tr1", "ps_tr0")
            join()
            return ln3T
        ln3T = stage_s7()
        S.paranoid = 's8a' in paranoid
        def stage_s8a():
            comb = M.sbuf("comb", [128, NTC * NE], F32, side="left")
            c8 = {n: M.sbuf("c8_" + n, [128, NE], F32, side="left")
                  for n in ("ex", "sc", "tmp")}
            c8u = M.sbuf("c8u", [128, NE], U32, side="left")
            c1 = {n: M.sbuf("c1_" + n, [128, 1], F32, side="left")
                  for n in ("nm", "sm", "rc", "m1", "m2", "nb", "z1")}
            ps_lg = M.psum("ps_lg", [128, NE], F32)
            gwb_ready = S.val(s_pool)
            lg_free = 0
            S.op("vector", lambda e: e.memset(c1["nb"][:], NEG_BIG),
                 waits=[(s_join, S.val(s_join))], inc=s_dve)
            S.op("vector", lambda e: e.memset(c1["z1"][:], 0.0), inc=s_dve)
            for tc in range(NTC):
                pairs = [(ln3T[:, c * TBLK + tc * 128: c * TBLK + (tc + 1) * 128],
                          gwb[:, c * NE:(c + 1) * NE]) for c in range(NDC)]
                pv = mm_group(ps_lg[:], pairs,
                              extra_first_waits=[(s_pool, gwb_ready),
                                                 (s_act, lg_free),
                                                 (s_join, S.val(s_join))])
                dv = S.op("vector", lambda e: e.reduce_max(
                    c1["nm"][:], ps_lg[:], axis=AX.X, negate=True),
                    waits=[(s_pe, pv), (s_dve, S.val(s_dve))], inc=s_dve)
                av = S.op("scalar", lambda e: e.activation(
                    out=c8["ex"][:], in_=ps_lg[:], func=AF.Exp, bias=c1["nm"][:],
                    accum_out=c1["sm"][:]),
                    waits=[(s_dve, dv), (s_act, S.val(s_act))], inc=s_act)
                lg_free = av
                dv = S.op("vector", lambda e: e.reciprocal(c1["rc"][:], c1["sm"][:]),
                          waits=[(s_act, av)], inc=s_dve)
                av = S.op("scalar", lambda e: e.activation(
                    out=c8["sc"][:], in_=c8["ex"][:], func=AF.Copy,
                    scale=c1["rc"][:]), waits=[(s_dve, dv)], inc=s_act)
                dv = S.op("vector", lambda e: e.reduce_max(
                    c1["m1"][:], c8["sc"][:], axis=AX.X), waits=[(s_act, av)],
                    inc=s_dve)
                dv = S.op("vector", lambda e: e.tensor_scalar(
                    out=c8["tmp"][:], in0=c8["sc"][:], scalar1=c1["m1"][:],
                    scalar2=None, op0=ALU.subtract), waits=[(s_dve, dv)], inc=s_dve)
                dv = S.op("vector", lambda e: e.tensor_scalar(
                    out=c8u[:], in0=c8["tmp"][:], scalar1=0.0, scalar2=None,
                    op0=ALU.is_ge), waits=[(s_dve, dv)], inc=s_dve)
                dv = S.op("vector", lambda e: e.tensor_copy(
                    out=c8["tmp"][:], in_=c8["sc"][:]), waits=[(s_dve, dv)],
                    inc=s_dve)
                dv = S.op("vector", lambda e: e.copy_predicated(
                    c8["tmp"][:], c8u[:], c1["nb"][:].broadcast_to([128, NE])),
                    waits=[(s_dve, dv)], inc=s_dve)
                dv = S.op("vector", lambda e: e.reduce_max(
                    c1["m2"][:], c8["tmp"][:], axis=AX.X), waits=[(s_dve, dv)],
                    inc=s_dve)
                dv = S.op("vector", lambda e: e.tensor_scalar(
                    out=c8["tmp"][:], in0=c8["sc"][:], scalar1=c1["m2"][:],
                    scalar2=None, op0=ALU.subtract), waits=[(s_dve, dv)], inc=s_dve)
                dv = S.op("vector", lambda e: e.tensor_scalar(
                    out=c8u[:], in0=c8["tmp"][:], scalar1=0.0, scalar2=None,
                    op0=ALU.is_ge), waits=[(s_dve, dv)], inc=s_dve)
                dv = S.op("vector", lambda e, tc=tc: e.memset(
                    comb[:, tc * NE:(tc + 1) * NE], 0.0), waits=[(s_dve, dv)],
                    inc=s_dve)
                dv = S.op("vector", lambda e, tc=tc: e.copy_predicated(
                    comb[:, tc * NE:(tc + 1) * NE], c8u[:], c8["sc"][:]),
                    waits=[(s_dve, dv)], inc=s_dve)
            comb_done = S.val(s_dve)
            M.free("ps_lg")
            return comb, comb_done
        comb, comb_done = stage_s8a()
        S.paranoid = 's8b' in paranoid
        def stage_s8b():
            gbuf = M.sbuf("gbuf", [128, NHC * TBLK], BF16, side="left")
            racc = M.sbuf("racc", [128, NTC * D], F32, side="right")
            sgt = [M.sbuf(f"sgt{i}", [128, 512], BF16, side="left") for i in range(2)]
            ws_gu = WStream("gu", mgu_d, D, 256, 8)
            ws_dn = WStream("dn", mdown_d, HM, 256, 6)
            ps_g = [M.psum(f"ps_g{i}", [128, 512], F32) for i in range(2)]
            ps_u = [M.psum(f"ps_u{i}", [128, 512], F32) for i in range(2)]
            ps_dn = [M.psum(f"ps_dn{i}", [128, 256], F32) for i in range(2)]
            g_free = [0, 0]
            u_free = [0, 0]
            gb_free = 0
            dn_free = [0, 0]
            racc_free = 0

            for ex in range(NE):
                for og in range(NHC):
                    bi = ws_gu.load(og * 256, rc0=ex * NDC)
                    wready = ws_gu.bf_ready[bi]
                    last_pe = None
                    for ts in range(2):
                        pi = (og * 2 + ts) % 2
                        gpairs = [(ws_gu.bf[bi][:, c * 256: c * 256 + 128],
                                   ln3T[:, c * TBLK + ts * 512:
                                        c * TBLK + (ts + 1) * 512])
                                  for c in range(NDC)]
                        upairs = [(ws_gu.bf[bi][:, c * 256 + 128: c * 256 + 256],
                                   ln3T[:, c * TBLK + ts * 512:
                                        c * TBLK + (ts + 1) * 512])
                                  for c in range(NDC)]
                        pvg = mm_group(ps_g[pi][:], gpairs,
                                       extra_first_waits=[(s_act, wready),
                                                          (s_act, g_free[pi]),
                                                          (s_pe, gb_free),
                                                          (s_dve, comb_done),
                                                          (s_join, S.val(s_join))])
                        pvu = mm_group(ps_u[pi][:], upairs,
                                       extra_first_waits=[(s_dve, u_free[pi])])
                        last_pe = pvu
                        av = S.op("scalar", lambda e, pi=pi: e.activation(
                            out=sgt[pi][:], in_=ps_g[pi][:], func=AF.Silu),
                            waits=[(s_pe, pvg), (s_dve, u_free[pi])], inc=s_act)
                        g_free[pi] = av
                        dst = gbuf[:, og * TBLK + ts * 512:
                                   og * TBLK + (ts + 1) * 512]
                        dv = S.op("vector", lambda e, pi=pi, d=dst:
                                  e.tensor_tensor(out=d, in0=ps_u[pi][:],
                                                  in1=sgt[pi][:], op=ALU.mult),
                                  waits=[(s_pe, pvu), (s_act, av)], inc=s_dve)
                        u_free[pi] = dv
                    ws_gu.consumed(bi, last_pe)
                gu_done = S.val(s_dve)
                for oh in range(8):
                    bi = ws_dn.load(oh * 256, rc0=ex * NHC, nrc=NHC)
                    wready = ws_dn.bf_ready[bi]
                    last_pe = None
                    for tc in range(NTC):
                        pi = (oh * NTC + tc) % 2
                        pairs = [(gbuf[:, hcc * TBLK + tc * 128:
                                       hcc * TBLK + (tc + 1) * 128],
                                  ws_dn.bf[bi][:, hcc * 256:(hcc + 1) * 256])
                                 for hcc in range(NHC)]
                        pv = mm_group(ps_dn[pi][:], pairs,
                                      extra_first_waits=[(s_dve, gu_done),
                                                         (s_act, wready),
                                                         (s_dve, dn_free[pi])])
                        last_pe = pv
                        racc_sl = racc[:, tc * D + oh * 256:
                                       tc * D + (oh + 1) * 256]
                        csl = comb[:, tc * NE + ex: tc * NE + ex + 1]
                        if ex == 0:
                            dv = S.op("vector", lambda e, pi=pi, r=racc_sl, c=csl:
                                      e.tensor_scalar(out=r, in0=ps_dn[pi][:],
                                                      scalar1=c, scalar2=None,
                                                      op0=ALU.mult),
                                      waits=[(s_pe, pv), (s_dve, comb_done),
                                             (s_dve, racc_free)], inc=s_dve)
                        else:
                            dv = S.op("vector", lambda e, pi=pi, r=racc_sl, c=csl:
                                      e.scalar_tensor_tensor(
                                          out=r, in0=ps_dn[pi][:], scalar=c,
                                          in1=r, op0=ALU.mult, op1=ALU.add),
                                      waits=[(s_pe, pv)], inc=s_dve)
                        dn_free[pi] = dv
                    ws_dn.consumed(bi, last_pe)
                gb_free = S.val(s_pe)
            ws_dn.done()
            ws_gu.done()
            M.free("sgt1", "sgt0")
            M.free("gbuf")
            M.free("ps_dn1", "ps_dn0", "ps_u1", "ps_u0", "ps_g1", "ps_g0")
            racc_done = S.val(s_dve)
            join()
            return racc, racc_done, dn_free, gb_free
        racc, racc_done, dn_free, gb_free = stage_s8b()
        S.paranoid = 's9' in paranoid
        def stage_s9():
            svSh = M.sbuf("svSh", [128, NSC * TBLK], BF16, side="left")
            sgt = [M.sbuf(f"sgt{i}", [128, 512], BF16, side="left") for i in range(2)]
            ws_sgu = WStream("sgu", swgu_d, D, 256, 8)
            ps_g = [M.psum(f"ps_g{i}", [128, 512], F32) for i in range(2)]
            ps_u = [M.psum(f"ps_u{i}", [128, 512], F32) for i in range(2)]
            ps_dn = [M.psum(f"ps_dn{i}", [128, 256], F32) for i in range(2)]
            dn_free = [0, 0]
            g_free = [0, 0]
            u_free = [0, 0]
            for og in range(NSC):
                bi = ws_sgu.load(og * 256)
                wready = ws_sgu.bf_ready[bi]
                last_pe = None
                for ts in range(2):
                    pi = (og * 2 + ts) % 2
                    gpairs = [(ws_sgu.bf[bi][:, c * 256: c * 256 + 128],
                               ln3T[:, c * TBLK + ts * 512:
                                    c * TBLK + (ts + 1) * 512])
                              for c in range(NDC)]
                    upairs = [(ws_sgu.bf[bi][:, c * 256 + 128: c * 256 + 256],
                               ln3T[:, c * TBLK + ts * 512:
                                    c * TBLK + (ts + 1) * 512])
                              for c in range(NDC)]
                    pvg = mm_group(ps_g[pi][:], gpairs,
                                   extra_first_waits=[(s_act, wready),
                                                      (s_act, g_free[pi]),
                                                      (s_pe, gb_free)])
                    pvu = mm_group(ps_u[pi][:], upairs,
                                   extra_first_waits=[(s_dve, u_free[pi])])
                    last_pe = pvu
                    av = S.op("scalar", lambda e, pi=pi: e.activation(
                        out=sgt[pi][:], in_=ps_g[pi][:], func=AF.Silu),
                        waits=[(s_pe, pvg), (s_dve, u_free[pi])], inc=s_act)
                    g_free[pi] = av
                    dst = svSh[:, og * TBLK + ts * 512: og * TBLK + (ts + 1) * 512]
                    dv = S.op("vector", lambda e, pi=pi, d=dst: e.tensor_tensor(
                        out=d, in0=ps_u[pi][:], in1=sgt[pi][:], op=ALU.mult),
                        waits=[(s_pe, pvu), (s_act, av)], inc=s_dve)
                    u_free[pi] = dv
                ws_sgu.consumed(bi, last_pe)
            ws_sgu.done()
            svsh_done = S.val(s_dve)
            ws_sd = WStream("sd", swd_d, HS, 256, 11)
            for oh in range(8):
                bi = ws_sd.load(oh * 256)
                wready = ws_sd.bf_ready[bi]
                last_pe = None
                for tc in range(NTC):
                    pi = (oh * NTC + tc) % 2
                    pairs = [(svSh[:, sc_ * TBLK + tc * 128:
                                  sc_ * TBLK + (tc + 1) * 128],
                              ws_sd.bf[bi][:, sc_ * 256:(sc_ + 1) * 256])
                             for sc_ in range(NSC)]
                    pv = mm_group(ps_dn[pi][:], pairs,
                                  extra_first_waits=[(s_dve, svsh_done),
                                                     (s_act, wready),
                                                     (s_dve, dn_free[pi])])
                    last_pe = pv
                    xi = tc % 2
                    xv = dma("scalar", xsl[xi][:, 0:256],
                             spill_d[row0 + tc * 128: row0 + (tc + 1) * 128,
                                     oh * 256:(oh + 1) * 256], s_x[xi],
                             waits=[(s_dve, xsl_free[xi])])
                    oi = (oh * NTC + tc) % 4
                    racc_sl = racc[:, tc * D + oh * 256: tc * D + (oh + 1) * 256]
                    dv = S.op("vector", lambda e, pi=pi, oi=oi, r=racc_sl:
                              e.tensor_add(outr[oi][:, 0:256], ps_dn[pi][:], r),
                              waits=[(s_pe, pv), (s_dve, racc_done),
                                     (s_o[oi], outr_free[oi])], inc=s_dve)
                    dn_free[pi] = dv
                    dv = S.op("vector", lambda e, oi=oi, xi=xi: e.tensor_add(
                        outr[oi][:, 0:256], outr[oi][:, 0:256], xsl[xi][:, 0:256]),
                        waits=[(s_x[xi], xv), (s_dve, dv)], inc=s_dve)
                    xsl_free[xi] = dv
                    yv = dma("sync", y_d[row0 + tc * 128: row0 + (tc + 1) * 128,
                                         oh * 256:(oh + 1) * 256],
                             outr[oi][:, 0:256], s_o[oi], waits=[(s_dve, dv)])
                    outr_free[oi] = yv
                ws_sd.consumed(bi, last_pe)
            ws_sd.done()
            M.free("sgt1", "sgt0", "svSh")
            M.free("ps_dn1", "ps_dn0", "ps_u1", "ps_u0", "ps_g1", "ps_g0")
            M.free("c1_z1", "c1_nb", "c1_m2", "c1_m1", "c1_rc", "c1_sm", "c1_nm")
            M.free("c8u", "c8_tmp", "c8_sc", "c8_ex", "comb")
            M.free("racc", "ln3T")
            join()
            return None
        stage_s9()

    S.paranoid = False
    for i in range(4):
        S.op("sync", lambda e, i=i, v=S.val(s_o[i]): e.wait_ge(s_o[i], v))

    stats = S.emit(block)
    block_guard.__exit__(None, None, None)
    total = sum(stats.values())
    print(f"[kernel] instruction ops per engine: {stats} total={total}")
    return nc


def _host_consts():
    identb = np.eye(128, dtype=ml_dtypes.bfloat16)
    maskt = np.triu(np.full((128, 128), NEG_BIG, dtype=np.float32), k=1)
    onesf = np.ones((128, 128), dtype=np.float32)
    return identb, maskt, onesf


def _rope_tables(core):
    flat = np.arange(core * T_CORE, (core + 1) * T_CORE, dtype=np.int64)
    s = flat % 4096
    inv = 1.0 / (10000.0 ** (np.arange(0, DH, 2, dtype=np.float64) / DH))
    ang = np.outer(s.astype(np.float64), inv)
    cosc = np.concatenate([np.cos(ang), np.cos(ang)], axis=1).astype(np.float32)
    sintc = np.concatenate([-np.sin(ang), np.sin(ang)], axis=1).astype(np.float32)
    return cosc, sintc


def _interleave_pairs(a, b):
    D_, N = a.shape
    nch = N // 128
    out = np.empty((D_, 2 * N), dtype=a.dtype)
    av = a.reshape(D_, nch, 128)
    bv = b.reshape(D_, nch, 128)
    o = out.reshape(D_, nch, 2, 128)
    o[:, :, 0, :] = av
    o[:, :, 1, :] = bv
    return out


def kernel(**inputs):
    return _kernel_impl(inputs, OUT_STAGE)


def _kernel_impl(inputs, out_stage, paranoid=()):
    x = np.ascontiguousarray(
        np.asarray(inputs["L0_attn_x"], dtype=np.float32).reshape(-1, D))
    f32 = lambda k: np.ascontiguousarray(np.asarray(inputs[k], dtype=np.float32))

    wqkvT = np.ascontiguousarray(f32("L0_attn_w_qkv").T)
    woT = np.ascontiguousarray(f32("L0_attn_w_o").T)
    wup = f32("L0_attn_w_up")
    wupT_g = np.ascontiguousarray(wup[:INTER].T)
    wupT_v = np.ascontiguousarray(wup[INTER:].T)
    wupT = _interleave_pairs(wupT_g, wupT_v)
    wdT = np.ascontiguousarray(f32("L0_attn_w_down").T)
    gatew = f32("L0_ffn_gate_w")
    mg = f32("L0_ffn_w_gate")
    mu = f32("L0_ffn_w_up")
    mgu = np.stack([_interleave_pairs(mg[e], mu[e]) for e in range(NE)])
    mgu = np.ascontiguousarray(mgu.reshape(NE * D, 2 * HM))
    mdown = np.ascontiguousarray(f32("L0_ffn_w_down").reshape(NE * HM, D))
    swgu = _interleave_pairs(f32("L0_ffn_sw_gate"), f32("L0_ffn_sw_up"))
    swd = f32("L0_ffn_sw_down")
    attnw_in = np.ascontiguousarray(
        f32("L0_attn_attn_norm_w").reshape(NDC, 128).T)
    ffnw_in = np.ascontiguousarray(f32("L0_attn_ffn_norm_w").reshape(NDC, 128).T)

    identb, maskt, onesf = _host_consts()

    nc = build_program(out_stage, paranoid)
    in_maps = []
    for core in range(NCORE):
        cosc, sintc = _rope_tables(core)
        in_maps.append(dict(
            x=np.ascontiguousarray(x[core * T_CORE:(core + 1) * T_CORE]),
            wqkvT=wqkvT, woT=woT, wupT=wupT, wdT=wdT, gatew=gatew,
            mgu=mgu, mdown=mdown, swgu=swgu, swd=swd,
            cosc=cosc, sintc=sintc, identb=identb, maskt=maskt,
            onesf=onesf, attnw_in=attnw_in, ffnw_in=ffnw_in,
        ))
    res = run_bass_kernel_spmd(nc, in_maps, core_ids=list(range(NCORE)))
    y = np.concatenate([r["y"] for r in res.results], axis=0)
    return y.reshape(4, 4096, D)


# version 09 (final)
# speedup vs baseline: 1.0721x; 1.0721x over previous
import numpy as np
import ml_dtypes
from contextlib import ExitStack

import concourse.bass as bass
import concourse.mybir as mybir
import concourse.bass_utils as _bass_utils
from concourse.bass_utils import run_bass_kernel_spmd

F32 = mybir.dt.float32
BF16 = mybir.dt.bfloat16
U32 = mybir.dt.uint32
AF = mybir.ActivationFunctionType
ALU = mybir.AluOpType
AX = mybir.AxisListType

NCORE = 8
T_CORE = 2048
NBLK = 2
TBLK = 1024
NTC = 8
D = 2048
NDC = 16
DH = 64
EPS = 1e-5
INTER = 4096
NIC = 32
NE = 8
HM = 1408
NHC = 11
HS = 2816
NSC = 22
NEG_BIG = -3.0e38

OUT_STAGE = "x2"


class Sched:
    ENGINES = ("sync", "tensor", "vector", "scalar", "gpsimd")

    def __init__(self, nc):
        self.nc = nc
        self.ops = []
        self.counts = {}
        self.paranoid = False
        self.all_sems = []
        self.own_sems = {}

    def sem(self, name):
        h = self.nc.alloc_semaphore(name=name)
        self.counts[name] = 0
        self.all_sems.append(h)
        return h

    def op(self, engine, fn, waits=(), inc=None, inc_amt=1):
        if self.paranoid:
            waits = list(waits) + [(s, self.counts[s.name])
                                   for s in self.all_sems]
        ws = [(s, v) for (s, v) in waits if v > 0]
        own = self.own_sems.get(engine)
        own_val = self.counts[own.name] if own is not None else 0
        self.ops.append((engine, fn, ws, inc, inc_amt, own, own_val))
        if inc is not None:
            self.counts[inc.name] += inc_amt
            return self.counts[inc.name]
        return None

    def bump(self, sem, amt=1):
        self.counts[sem.name] += amt
        return self.counts[sem.name]

    def val(self, sem):
        return self.counts[sem.name]

    def emit(self, block):
        stats = {e: 0 for e in self.ENGINES}

        def body(eng_name):
            def run(eng):
                seen = {}
                for engine, fn, waits, inc, inc_amt, own, own_val in self.ops:
                    if engine != eng_name:
                        continue
                    if own is not None and seen.get(own.name, -1) < own_val:
                        eng.wait_ge(own, own_val)
                        seen[own.name] = own_val
                    for (s, v) in waits:
                        if seen.get(s.name, -1) < v:
                            eng.wait_ge(s, v)
                            seen[s.name] = v
                    inst = fn(eng)
                    stats[eng_name] += 1
                    if inc is not None:
                        inst.then_inc(inc, inc_amt)
            return run

        block.sync(body("sync"))
        block.tensor(body("tensor"))
        block.vector(body("vector"))
        block.scalar(body("scalar"))
        block.gpsimd(body("gpsimd"))
        return stats


class Mem:
    def __init__(self, nc):
        self.nc = nc
        self.guards = {}
        self.serial = 0

    def _uniq(self, name):
        self.serial += 1
        return f"{name}__{self.serial}"

    def sbuf(self, name, shape, dtype, side="left"):
        g = self.nc.sbuf_tensor(self._uniq(name), shape, dtype, side=side)
        h = g.__enter__()
        self.guards[name] = g
        return h

    def psum(self, name, shape, dtype):
        g = self.nc.psum_tensor(self._uniq(name), shape, dtype)
        h = g.__enter__()
        self.guards[name] = g
        return h

    def free(self, *names):
        for name in names:
            self.guards.pop(name).__exit__(None, None, None)


def build_program(out_stage=OUT_STAGE, paranoid=()):
    nc = bass.Bass("TRN2", target_bir_lowering=False)

    def din(name, shape, dtype=F32):
        return nc.dram_tensor(name, shape, dtype, kind="ExternalInput").ap()

    x_d = din("x", [T_CORE, D])
    wqkvT_d = din("wqkvT", [D, 3 * D])
    woT_d = din("woT", [D, D])
    wupT_d = din("wupT", [D, 2 * INTER])
    wdT_d = din("wdT", [INTER, D])
    gatew_d = din("gatew", [D, NE])
    mgu_d = din("mgu", [NE * D, 2 * HM])
    mdown_d = din("mdown", [NE * HM, D])
    swgu_d = din("swgu", [D, 2 * HS])
    swd_d = din("swd", [HS, D])
    cosc_d = din("cosc", [T_CORE, DH])
    sintc_d = din("sintc", [T_CORE, DH])
    identb_d = din("identb", [128, 128], BF16)
    maskt_d = din("maskt", [128, 128])
    onesf_d = din("onesf", [128, 128])

    y_d = nc.dram_tensor("y", [T_CORE, D], F32, kind="ExternalOutput").ap()
    spill_d = nc.dram_tensor("spill", [T_CORE, D], F32, kind="Internal").ap()

    S = Sched(nc)
    M = Mem(nc)

    s_w = S.sem("s_w")
    s_c = S.sem("s_c")
    s_cs = S.sem("s_cs")
    s_x = [S.sem("s_x0"), S.sem("s_x1")]
    s_o = [S.sem(f"s_o{i}") for i in range(4)]
    s_pe = S.sem("s_pe")
    s_dve = S.sem("s_dve")
    s_act = S.sem("s_act")
    s_pool = S.sem("s_pool")
    s_join = S.sem("s_join")
    S.own_sems = {"vector": s_dve, "scalar": s_act, "gpsimd": s_pool}

    identb = M.sbuf("c_identb", [128, 128], BF16)
    maskt = M.sbuf("c_maskt", [128, 128], F32)
    onesf = M.sbuf("c_onesf", [128, 128], F32)
    attnw = M.sbuf("attnw", [128, NDC], F32)
    ffnw = M.sbuf("ffnw", [128, NDC], F32)
    gwf = M.sbuf("gwf", [128, NDC * NE], F32)
    gwb = M.sbuf("gwb", [128, NDC * NE], BF16)
    xsl = [M.sbuf(f"xsl{i}", [128, 512], F32) for i in range(2)]
    outr = [M.sbuf(f"outr{i}", [128, 512], F32) for i in range(4)]
    junk = M.sbuf("junk", [128, 512], F32)
    stn = ["sum1", "sq1", "m1v", "var1", "lnA", "r1", "lnV", "dd", "v2", "lnB",
           "lnT", "s12", "b12", "zb1", "sqs2", "r2b", "sum3", "sq3", "m3v",
           "var3", "ln3A", "s3", "b3", "tstat"]
    ST = {n: M.sbuf("st_" + n, [128, NTC], F32) for n in stn}
    sq4 = M.sbuf("sq4", [128, NTC * 4], F32)
    s34 = M.sbuf("s34", [128, 128], F32)
    nmax = [M.sbuf(f"nmax{i}", [128, 1], F32) for i in range(2)]
    bias8 = [M.sbuf(f"bias8{i}", [128, 1], F32) for i in range(2)]
    epsT = M.sbuf("epsT", [128, 1], F32)

    xsl_free = [0, 0]
    outr_free = [0, 0, 0, 0]

    JOIN_ENGINES = ("tensor", "vector", "scalar", "gpsimd")

    def join():
        own = {"tensor": s_pe, "vector": s_dve, "scalar": s_act,
               "gpsimd": s_pool}
        for e in JOIN_ENGINES:
            S.op(e, lambda eng, s=own[e], v=S.val(own[e]): eng.wait_ge(s, v))
            S.op(e, lambda eng: eng.sem_inc(s_join, 1))
            S.bump(s_join, 1)
        b = S.val(s_join)
        for e in JOIN_ENGINES:
            S.op(e, lambda eng, b=b: eng.wait_ge(s_join, b))
        return b

    def dma(engine, out, in_, sem, waits=()):
        return S.op(engine, lambda e, o=out, i=in_: e.dma_start(out=o, in_=i),
                    waits=waits, inc=sem, inc_amt=16)

    S.op("vector", lambda e: e.memset(epsT[:], EPS), inc=s_dve)
    for (dst, src) in ((identb, identb_d), (maskt, maskt_d),
                       (onesf, onesf_d)):
        dma("sync", dst[:], src, s_c)
    attnw_in_d = din("attnw_in", [128, NDC])
    ffnw_in_d = din("ffnw_in", [128, NDC])
    dma("sync", attnw[:], attnw_in_d, s_c)
    dma("sync", ffnw[:], ffnw_in_d, s_c)
    dma("sync", gwf[:].rearrange("p (c e) -> p c e", e=NE),
        gatew_d.rearrange("(c p) e -> p c e", p=128), s_c)
    v_consts = S.val(s_c)
    S.op("gpsimd", lambda e: e.tensor_copy(out=gwb[:], in_=gwf[:]),
         waits=[(s_c, v_consts)], inc=s_pool)

    class WStream:
        def __init__(self, tag, src_ap, src_rows, ncols_og, dgrp, scale=None,
                     nbuf=2, side="left"):
            self.tag = tag
            self.src = src_ap
            self.nrc = src_rows // 128
            self.ncols = ncols_og
            self.dgrp = dgrp
            self.scale = scale
            self.s_f = [S.sem(f"sw_{tag}_{M.serial}_{i}") for i in range(2)]
            self.fst = [M.sbuf(f"wf_{tag}{i}", [128, dgrp * ncols_og], F32,
                               side=side) for i in range(2)]
            self.bf = [M.sbuf(f"wb_{tag}{i}", [128, self.nrc * ncols_og], BF16,
                              side=side) for i in range(nbuf)]
            self.nbuf = nbuf
            self.fst_free = [S.val(s_act), S.val(s_act)]
            self.bf_free = [S.val(s_pe)] * nbuf
            self.bf_ready = [0] * nbuf
            self.n = 0
            self.first_wait = [(s_join, S.val(s_join)),
                               (s_pool, S.val(s_pool)),
                               (s_pe, S.val(s_pe)),
                               (s_dve, S.val(s_dve))]

        def load(self, col0, ncols=None, nrc=None, rc0=0):
            ncols = ncols or self.ncols
            nrc = nrc or self.nrc
            bi = self.n % self.nbuf
            ngrp = (nrc + self.dgrp - 1) // self.dgrp
            conv_waits = [(s_pe, self.bf_free[bi])]
            pv = 0
            for g in range(ngrp):
                r0 = g * self.dgrp
                nr = min(self.dgrp, nrc - r0)
                fi = (self.n * ngrp + g) % 2
                src = self.src.rearrange("(c p) o -> p c o", p=128)[
                    :, rc0 + r0: rc0 + r0 + nr, col0:col0 + ncols]
                dst = self.fst[fi][:, 0:nr * ncols].rearrange(
                    "p (c o) -> p c o", o=ncols)
                w = dma("sync", dst, src, self.s_f[fi],
                        waits=self.first_wait + [(s_act, self.fst_free[fi])])
                self.first_wait = []
                fsl = self.fst[fi][:, 0:nr * ncols]
                bsl = self.bf[bi][:, r0 * ncols:(r0 + nr) * ncols]
                pv = S.op("scalar", lambda e, o=bsl, i=fsl:
                          e.copy(out=o, in_=i),
                          waits=[(self.s_f[fi], w)] + conv_waits, inc=s_act)
                conv_waits = []
                self.fst_free[fi] = pv
            self.bf_ready[bi] = pv
            self.n += 1
            return bi

        def consumed(self, bi, pe_val):
            self.bf_free[bi] = pe_val

        def done(self):
            M.free(*(f"wb_{self.tag}{i}" for i in reversed(range(self.nbuf))))
            M.free(*(f"wf_{self.tag}{i}" for i in reversed(range(2))))

    def mm_group(psum_ap, pairs, extra_first_waits=()):
        n = len(pairs)
        v = None
        for i, (lt, rh) in enumerate(pairs):
            v = S.op("tensor",
                     lambda e, o=psum_ap, lt=lt, rh=rh, st=(i == 0),
                     sp=(i == n - 1): e.matmul(o, lt, rh, start=st, stop=sp),
                     waits=list(extra_first_waits) if i == 0 else [],
                     inc=s_pe if i == n - 1 else None)
        return v

    def transpose_to(ps, src_sl, dst_sl, waits=(), scale=None):
        pv = S.op("tensor", lambda e, ps=ps, s=src_sl:
                  e.transpose(ps[:], s, identb[:]), waits=waits, inc=s_pe)
        if scale is None:
            dv = S.op("vector", lambda e, ps=ps, d=dst_sl:
                      e.tensor_copy(out=d, in_=ps[:]), waits=[(s_pe, pv)],
                      inc=s_dve)
        else:
            dv = S.op("vector", lambda e, ps=ps, d=dst_sl, sc=scale:
                      e.tensor_scalar(out=d, in0=ps[:], scalar1=sc,
                                      scalar2=None, op0=ALU.mult),
                      waits=[(s_pe, pv)], inc=s_dve)
        return dv

    def ln_chain(tc, sum_b, sq_b, want_rms2):
        def sl(t):
            return t[:, tc:tc + 1]

        dv = S.op("vector", lambda e: e.tensor_scalar(
            out=sl(ST["m1v"]), in0=sl(sum_b), scalar1=1.0 / D, scalar2=None,
            op0=ALU.mult), waits=[(s_act, S.val(s_act))], inc=s_dve)
        dv = S.op("vector", lambda e: e.tensor_mul(
            sl(ST["tstat"]), sl(ST["m1v"]), sl(ST["m1v"])),
            waits=[(s_dve, dv)], inc=s_dve)
        dv = S.op("vector", lambda e: e.scalar_tensor_tensor(
            out=sl(ST["var1"]), in0=sl(sq_b), scalar=1.0 / D,
            in1=sl(ST["tstat"]), op0=ALU.mult, op1=ALU.subtract),
            waits=[(s_dve, dv)], inc=s_dve)
        av = S.op("scalar", lambda e: e.activation(
            out=sl(ST["lnA"]), in_=sl(ST["var1"]), func=AF.Ln, bias=epsT[:]),
            waits=[(s_dve, dv)], inc=s_act)
        av = S.op("scalar", lambda e: e.activation(
            out=sl(ST["r1"]), in_=sl(ST["lnA"]), func=AF.Exp, scale=-0.5),
            waits=[(s_act, av)], inc=s_act)
        if want_rms2:
            av = S.op("scalar", lambda e: e.activation(
                out=sl(ST["lnV"]), in_=sl(ST["var1"]), func=AF.Ln),
                waits=[(s_act, av)], inc=s_act)
            dv = S.op("vector", lambda e: e.tensor_sub(
                sl(ST["dd"]), sl(ST["lnV"]), sl(ST["lnA"])),
                waits=[(s_act, av)], inc=s_dve)
            av = S.op("scalar", lambda e: e.activation(
                out=sl(ST["v2"]), in_=sl(ST["dd"]), func=AF.Exp),
                waits=[(s_dve, dv)], inc=s_act)
            av = S.op("scalar", lambda e: e.activation(
                out=sl(ST["lnB"]), in_=sl(ST["v2"]), func=AF.Ln, bias=epsT[:]),
                waits=[(s_act, av)], inc=s_act)
            dv = S.op("vector", lambda e: e.tensor_add(
                sl(ST["lnT"]), sl(ST["lnA"]), sl(ST["lnB"])),
                waits=[(s_act, av)], inc=s_dve)
            av = S.op("scalar", lambda e: e.activation(
                out=sl(ST["s12"]), in_=sl(ST["lnT"]), func=AF.Exp, scale=-0.5),
                waits=[(s_dve, dv)], inc=s_act)
            dv = S.op("vector", lambda e: e.scalar_tensor_tensor(
                out=sl(ST["b12"]), in0=sl(ST["m1v"]), scalar=-1.0,
                in1=sl(ST["s12"]), op0=ALU.mult, op1=ALU.mult),
                waits=[(s_act, av)], inc=s_dve)
        dv = S.op("vector", lambda e: e.scalar_tensor_tensor(
            out=sl(ST["zb1"]), in0=sl(ST["m1v"]), scalar=-1.0,
            in1=sl(ST["r1"]), op0=ALU.mult, op1=ALU.mult),
            waits=[(s_act, S.val(s_act)), (s_dve, dv)], inc=s_dve)
        return dv

    block_guard = nc.Block()
    block = block_guard.__enter__()

    for blk in range(NBLK):
        row0 = blk * TBLK
        S.paranoid = 's1' in paranoid
        def stage_s1():
            h1T = M.sbuf("h1T", [128, NDC * TBLK], BF16, side="left")
            scratch = M.sbuf("scratch", [128, D], F32, side="left")
            h1tile = M.sbuf("h1tile", [128, D], BF16, side="left")
            xr = [M.sbuf(f"xr{i}", [128, D], F32, side="left") for i in range(2)]
            xr_free = [[], []]
            ps_tr = [M.psum(f"ps_tr{i}", [128, 128], BF16) for i in range(2)]
            tr_free = [0, 0]

            h1_busy = 0
            for tc in range(NTC):
                ri = tc % 2
                xv = dma("scalar", xr[ri][:],
                         x_d[row0 + tc * 128: row0 + (tc + 1) * 128, :], s_x[ri],
                         waits=list(xr_free[ri]) + [(s_join, S.val(s_join))])
                av = S.op("scalar", lambda e, ri=ri, tc=tc: e.activation(
                    out=scratch[:], in_=xr[ri][:], func=AF.Copy,
                    accum_out=ST["sum1"][:, tc:tc + 1]),
                    waits=[(s_x[ri], xv)], inc=s_act)
                av = S.op("scalar", lambda e, ri=ri, tc=tc: e.activation(
                    out=scratch[:], in_=xr[ri][:], func=AF.Square,
                    accum_out=ST["sq1"][:, tc:tc + 1]),
                    waits=[(s_act, av)], inc=s_act)
                dv = ln_chain(tc, ST["sum1"], ST["sq1"], want_rms2=True)
                pl = S.op("gpsimd", lambda e, ri=ri, tc=tc: e.tensor_scalar(
                    out=h1tile[:], in0=xr[ri][:],
                    scalar1=ST["s12"][:, tc:tc + 1],
                    scalar2=ST["b12"][:, tc:tc + 1],
                    op0=ALU.mult, op1=ALU.add),
                    waits=[(s_dve, dv), (s_pool, S.val(s_pool)),
                           (s_pe, h1_busy)], inc=s_pool)
                xr_free[ri] = [(s_act, av), (s_pool, pl)]
                for c in range(NDC):
                    ti = (tc * NDC + c) % 2
                    dv2 = transpose_to(
                        ps_tr[ti], h1tile[:, c * 128:(c + 1) * 128],
                        h1T[:, c * TBLK + tc * 128: c * TBLK + (tc + 1) * 128],
                        waits=[(s_pool, pl), (s_dve, tr_free[ti])],
                        scale=attnw[:, c:c + 1])
                    tr_free[ti] = dv2
                h1_busy = S.val(s_pe)
            M.free("xr1", "xr0", "h1tile", "scratch")
            M.free("ps_tr1", "ps_tr0")
            join()
            return h1T
        h1T = stage_s1()
        S.paranoid = 's2' in paranoid
        def stage_s2():
            cosT = M.sbuf("cosT", [128, NTC * DH], F32, side="left")
            sintT = M.sbuf("sintT", [128, NTC * DH], F32, side="left")
            dma("sync", cosT[:].rearrange("p (c j) -> p c j", j=DH),
                cosc_d.rearrange("(c p) j -> p c j", p=128)[:, blk * NTC:(blk + 1) * NTC, :],
                s_cs, waits=[(s_join, S.val(s_join))])
            dma("sync", sintT[:].rearrange("p (c j) -> p c j", j=DH),
                sintc_d.rearrange("(c p) j -> p c j", p=128)[:, blk * NTC:(blk + 1) * NTC, :],
                s_cs)
            cs_ready = S.val(s_cs)
            q_tok = M.sbuf("q_tok", [128, NTC * D], BF16, side="right")
            k_tok = M.sbuf("k_tok", [128, NTC * D], BF16, side="right")
            vtok = M.sbuf("vtok", [128, NTC * 2112], BF16, side="right")
            rt1 = M.sbuf("rt1", [128, 512], F32, side="left")
            rt2 = M.sbuf("rt2", [128, 512], F32, side="left")
            ws_qkv = WStream("qkv", wqkvT_d, D, 512, 2)
            ps_mm = [M.psum(f"ps_mm{i}", [128, 512], F32) for i in range(2)]
            mm_free = [0, 0]

            S.op("gpsimd", lambda e: e.memset(
                vtok[:].rearrange("p (g x) -> p g x", x=66)[:, :, 64:65], 1.0),
                waits=[(s_join, S.val(s_join))], inc=s_pool)
            S.op("gpsimd", lambda e: e.memset(
                vtok[:].rearrange("p (g x) -> p g x", x=66)[:, :, 65:66], 0.0),
                inc=s_pool)
            vinit = S.val(s_pool)

            for og in range(12):
                bi = ws_qkv.load(og * 512)
                wready = ws_qkv.bf_ready[bi]
                last_pe = None
                for tc in range(NTC):
                    pi = (og * NTC + tc) % 2
                    pairs = [(h1T[:, c * TBLK + tc * 128: c * TBLK + (tc + 1) * 128],
                              ws_qkv.bf[bi][:, c * 512:(c + 1) * 512])
                             for c in range(NDC)]
                    pv = mm_group(ps_mm[pi][:], pairs,
                                  extra_first_waits=[(s_act, wready),
                                                     (s_dve, mm_free[pi]),
                                                     (s_join, S.val(s_join))])
                    last_pe = pv
                    if og < 8:
                        dstbuf = q_tok if og < 4 else k_tok
                        col0 = (og % 4) * 512
                        gtc = tc
                        ps3 = ps_mm[pi][:].rearrange("p (h j) -> p h j", j=DH)
                        t13 = rt1[:].rearrange("p (h j) -> p h j", j=DH)
                        t23 = rt2[:].rearrange("p (h j) -> p h j", j=DH)
                        cossl = cosT[:, gtc * DH:(gtc + 1) * DH]
                        sinsl = sintT[:, gtc * DH:(gtc + 1) * DH]
                        dv = S.op("vector", lambda e, ps3=ps3, t13=t13, c=cossl:
                                  e.tensor_tensor(
                                      out=t13, in0=ps3,
                                      in1=c[:, None, :].broadcast_to([128, 8, DH]),
                                      op=ALU.mult),
                                  waits=[(s_pe, pv), (s_dve, S.val(s_dve)),
                                         (s_cs, cs_ready)],
                                  inc=s_dve)
                        dv = S.op("vector", lambda e, ps3=ps3, t23=t23, s=sinsl:
                                  e.tensor_tensor(
                                      out=t23[:, :, 0:32], in0=ps3[:, :, 32:64],
                                      in1=s[:, None, 0:32].broadcast_to([128, 8, 32]),
                                      op=ALU.mult), inc=s_dve)
                        dv = S.op("vector", lambda e, ps3=ps3, t23=t23, s=sinsl:
                                  e.tensor_tensor(
                                      out=t23[:, :, 32:64], in0=ps3[:, :, 0:32],
                                      in1=s[:, None, 32:64].broadcast_to([128, 8, 32]),
                                      op=ALU.mult), inc=s_dve)
                        mm_free[pi] = dv
                        dv = S.op("vector", lambda e, d=dstbuf, tc=tc, col0=col0:
                                  e.tensor_add(
                                      d[:, tc * D + col0: tc * D + col0 + 512],
                                      rt1[:], rt2[:]),
                                  waits=[(s_dve, dv)], inc=s_dve)
                    else:
                        h0 = (og - 8) * 8
                        dst = vtok[:, tc * 2112 + h0 * 66: tc * 2112 + (h0 + 8) * 66] \
                            .rearrange("p (h x) -> p h x", x=66)[:, :, 0:64]
                        src = ps_mm[pi][:].rearrange("p (h j) -> p h j", j=DH)
                        dv = S.op("vector", lambda e, d=dst, s=src:
                                  e.tensor_copy(out=d, in_=s),
                                  waits=[(s_pe, pv), (s_pool, vinit)], inc=s_dve)
                        mm_free[pi] = dv
                ws_qkv.consumed(bi, last_pe)
            ws_qkv.done()
            M.free("rt2", "rt1", "sintT", "cosT")
            M.free("h1T")
            M.free("ps_mm1", "ps_mm0")
            join()
            return q_tok, k_tok, vtok
        q_tok, k_tok, vtok = stage_s2()
        S.paranoid = 's3' in paranoid
        def stage_s3():
            ctxT = M.sbuf("ctxT", [128, NDC * TBLK], BF16, side="left")
            qTp = [M.sbuf(f"qTp{i}", [128, TBLK], BF16, side="left") for i in range(2)]
            kTp = [M.sbuf(f"kTp{i}", [128, TBLK], BF16, side="left") for i in range(2)]
            praw = [M.sbuf(f"praw{i}", [128, TBLK], F32, side="left") for i in range(2)]
            probs = [M.sbuf(f"probs{i}", [128, TBLK], BF16, side="left") for i in range(2)]
            probsT = [M.sbuf(f"probsT{i}", [128, 4608], BF16, side="left")
                      for i in range(2)]
            sums = [M.sbuf(f"sums{i}", [128, 1], F32, side="left") for i in range(2)]
            rmax = [M.sbuf(f"rmax{i}", [128, 1], F32, side="left") for i in range(2)]
            rcp = [M.sbuf(f"rcp{i}", [128, 1], F32, side="left") for i in range(2)]
            ps_tr = [M.psum(f"ps_tr{i}", [128, 128], BF16) for i in range(2)]
            ps_sc = [M.psum(f"ps_sc{i}", [128, TBLK], F32) for i in range(2)]
            ps_ctx = M.psum("ps_ctx", [64, TBLK], F32)
            tr_free = [0, 0]
            probs_free = [0, 0]
            probsT_free = [0, 0]
            sc_free = [0, 0]
            praw_free = [0, 0]
            ctx_free = 0

            def offp(c):
                return c * TBLK - 128 * (c * (c - 1) // 2)

            for pc in range(NDC):
                pb = pc % 2
                w0 = [(s_join, S.val(s_join))]
                for tc in range(NTC):
                    for (srcb, dstb) in ((q_tok, qTp[pb]), (k_tok, kTp[pb])):
                        ti = tc % 2
                        dv2 = transpose_to(
                            ps_tr[ti],
                            srcb[:, tc * D + pc * 128: tc * D + (pc + 1) * 128],
                            dstb[:, tc * 128:(tc + 1) * 128],
                            waits=w0 + [(s_dve, tr_free[ti])])
                        w0 = []
                        tr_free[ti] = dv2
                qk_ready = S.val(s_dve)
                trdone_h = [0, 0]
                for qi in range(NTC):
                    ext = (qi + 1) * 128
                    av_h = [0, 0]
                    for hh in range(2):
                        par = hh
                        b0 = hh * 64
                        lt = qTp[pb][b0:b0 + 64, qi * 128:(qi + 1) * 128]
                        segs = [(o, min(ext, o + 512)) for o in range(0, ext, 512)]
                        pv = None
                        for (a, b) in segs:
                            pv = S.op("tensor", lambda e, lt=lt, a=a, b=b, pb=pb,
                                      b0=b0, par=par:
                                      e.matmul(ps_sc[par][:, a:b], lt,
                                               kTp[pb][b0:b0 + 64, a:b],
                                               start=True, stop=True),
                                      waits=([(s_dve, qk_ready),
                                              (s_act, sc_free[par])]
                                             if a == 0 else []),
                                      inc=s_pe if b == ext else None)
                        dv = S.op("vector", lambda e, qi=qi, ext=ext, par=par:
                                  e.tensor_tensor(
                                      out=ps_sc[par][:, qi * 128:ext],
                                      in0=ps_sc[par][:, qi * 128:ext],
                                      in1=maskt[:], op=ALU.add),
                                  waits=[(s_pe, pv)], inc=s_dve)
                        dv = S.op("vector", lambda e, qi=qi, ext=ext, par=par:
                                  e.reduce_max(nmax[par][:],
                                               ps_sc[par][:, qi * 128:ext],
                                               axis=AX.X),
                                  waits=[(s_dve, dv),
                                         (s_act, sc_free[par])],
                                  inc=s_dve)
                        if qi == 0:
                            dv = S.op("vector", lambda e, par=par:
                                      e.tensor_copy(out=rmax[par][:],
                                                    in_=nmax[par][:]),
                                      waits=[(s_dve, dv)], inc=s_dve)
                        else:
                            dv = S.op("vector", lambda e, par=par:
                                      e.tensor_max(rmax[par][:], rmax[par][:],
                                                   nmax[par][:]),
                                      waits=[(s_dve, dv)], inc=s_dve)
                        dv = S.op("vector", lambda e, par=par: e.tensor_scalar(
                            out=bias8[par][:], in0=rmax[par][:], scalar1=-0.125,
                            scalar2=None, op0=ALU.mult), waits=[(s_dve, dv)],
                            inc=s_dve)
                        av = S.op("scalar", lambda e, ext=ext, par=par:
                                  e.activation(
                                      out=praw[par][:, 0:ext],
                                      in_=ps_sc[par][:, 0:ext],
                                      func=AF.Exp, bias=bias8[par][:],
                                      scale=0.125, accum_out=sums[par][:]),
                                  waits=[(s_dve, dv), (s_act, praw_free[par])],
                                  inc=s_act)
                        sc_free[par] = av
                        dv = S.op("vector", lambda e, par=par: e.reciprocal(
                            rcp[par][:], sums[par][:]), waits=[(s_act, av)],
                            inc=s_dve)
                        av = S.op("scalar", lambda e, ext=ext, par=par:
                                  e.activation(
                                      out=probs[par][:, 0:ext],
                                      in_=praw[par][:, 0:ext], func=AF.Copy,
                                      scale=rcp[par][:]),
                                  waits=[(s_dve, dv), (s_pe, probs_free[par])],
                                  inc=s_act)
                        praw_free[par] = av
                        av_h[hh] = av
                    for hh in range(2):
                        par = hh
                        av = av_h[hh]
                        for c in range(qi + 1):
                            ti = c % 2
                            pv2 = S.op("tensor", lambda e, par=par, c=c, ti=ti:
                                       e.transpose(
                                           ps_tr[ti][:],
                                           probs[par][:, c * 128:(c + 1) * 128],
                                           identb[:]),
                                       waits=[(s_act, av), (s_dve, tr_free[ti])],
                                       inc=s_pe)
                            probs_free[par] = pv2
                            dst = probsT[par][:, offp(c) + (qi - c) * 128:
                                              offp(c) + (qi - c) * 128 + 128]
                            dv2 = S.op("vector", lambda e, d=dst, ti=ti:
                                       e.tensor_copy(out=d, in_=ps_tr[ti][:]),
                                       waits=[(s_pe, pv2),
                                              (s_pe, probsT_free[par])],
                                       inc=s_dve)
                            tr_free[ti] = dv2
                        trdone_h[hh] = S.val(s_dve)
                for hh in range(2):
                    h = 2 * pc + hh
                    b0 = hh * 64
                    hb = hh
                    pv = None
                    for cc in range(NTC):
                        lt = vtok[:, cc * 2112 + h * 66: cc * 2112 + h * 66 + 64]
                        lo = cc * 128
                        segs = [(lo, 512), (512, TBLK)] if lo < 512 else [(lo, TBLK)]
                        for (a, b) in segs:
                            is_start = (cc == 0)
                            is_stop = (cc == 3 and b == 512) or \
                                      (cc == 7 and b == TBLK)
                            pv = S.op("tensor", lambda e, lt=lt, a=a, b=b, cc=cc,
                                      st=is_start, sp=is_stop, hb=hb:
                                      e.matmul(ps_ctx[:, a:b], lt,
                                               probsT[hb][:, offp(cc) + a - cc * 128:
                                                          offp(cc) + b - cc * 128],
                                               start=st, stop=sp),
                                      waits=([(s_dve, trdone_h[hh]),
                                              (s_dve, ctx_free)]
                                             if (cc == 0 and a == 0) else []),
                                      inc=s_pe)
                    probsT_free[hb] = pv
                    dv = S.op("vector", lambda e, pc=pc, b0=b0: e.tensor_copy(
                        out=ctxT[b0:b0 + 64, pc * TBLK:(pc + 1) * TBLK],
                        in_=ps_ctx[:]), waits=[(s_pe, pv)], inc=s_dve)
                    ctx_free = dv
            M.free("ps_ctx", "ps_sc1", "ps_sc0", "ps_tr1", "ps_tr0")
            M.free("rcp1", "rcp0", "rmax1", "rmax0", "sums1", "sums0", "probsT1", "probsT0",
                   "probs1", "probs0", "praw1", "praw0",
                   "kTp1", "kTp0", "qTp1", "qTp0")
            M.free("vtok", "k_tok", "q_tok")
            join()
            return ctxT
        ctxT = stage_s3()
        S.paranoid = 's4' in paranoid
        def stage_s4():
            RES = M.sbuf("RES", [128, NTC * D], F32, side="right")
            ztmp = [M.sbuf(f"ztmp{i}", [128, 512], F32, side="left") for i in range(2)]
            ws_wo = WStream("wo", woT_d, D, 512, 2)
            ps_mm = [M.psum(f"ps_mm{i}", [128, 512], F32) for i in range(2)]
            mm_free = [0, 0]
            for og in range(4):
                bi = ws_wo.load(og * 512)
                wready = ws_wo.bf_ready[bi]
                last_pe = None
                for tc in range(NTC):
                    pi = (og * NTC + tc) % 2
                    pairs = [(ctxT[:, c * TBLK + tc * 128: c * TBLK + (tc + 1) * 128],
                              ws_wo.bf[bi][:, c * 512:(c + 1) * 512])
                             for c in range(NDC)]
                    pv = mm_group(ps_mm[pi][:], pairs,
                                  extra_first_waits=[(s_act, wready),
                                                     (s_dve, mm_free[pi]),
                                                     (s_join, S.val(s_join))])
                    last_pe = pv
                    xi = tc % 2
                    xv = dma("scalar", xsl[xi][:],
                             x_d[row0 + tc * 128: row0 + (tc + 1) * 128,
                                 og * 512:(og + 1) * 512], s_x[xi],
                             waits=[(s_dve, xsl_free[xi])])
                    zi = tc % 2
                    zv = S.op("vector", lambda e, xi=xi, zi=zi, tc=tc:
                              e.tensor_scalar(
                                  out=ztmp[zi][:], in0=xsl[xi][:],
                                  scalar1=ST["r1"][:, tc:tc + 1],
                                  scalar2=ST["zb1"][:, tc:tc + 1],
                                  op0=ALU.mult, op1=ALU.add),
                              waits=[(s_x[xi], xv)], inc=s_dve)
                    res_sl = RES[:, tc * D + og * 512: tc * D + (og + 1) * 512]
                    dv = S.op("vector", lambda e, pi=pi, zi=zi, r=res_sl:
                              e.tensor_add(r, ps_mm[pi][:], ztmp[zi][:]),
                              waits=[(s_pe, pv)], inc=s_dve)
                    mm_free[pi] = dv
                    xsl_free[xi] = dv
                    av = S.op("scalar", lambda e, r=res_sl, tc=tc, og=og:
                              e.activation(
                                  out=junk[:], in_=r, func=AF.Square,
                                  accum_out=sq4[:, tc * 4 + og: tc * 4 + og + 1]),
                              waits=[(s_dve, dv), (s_act, S.val(s_act))], inc=s_act)
                ws_wo.consumed(bi, last_pe)
            ws_wo.done()
            M.free("ztmp1", "ztmp0")
            M.free("ctxT")
            h2T = M.sbuf("h2T", [128, NDC * TBLK], BF16, side="right")
            h2tile = M.sbuf("h2tile", [128, D], BF16, side="left")
            ps_trh = [M.psum(f"ps_trh{i}", [128, 128], BF16) for i in range(2)]
            trh_free = [0, 0]
            sq4done = S.val(s_act)
            h2_busy = 0
            for tc in range(NTC):
                dv = S.op("vector", lambda e, tc=tc: e.reduce_sum(
                    ST["sqs2"][:, tc:tc + 1], sq4[:, tc * 4:(tc + 1) * 4],
                    axis=AX.X), waits=[(s_act, sq4done)], inc=s_dve)
                av = S.op("scalar", lambda e, tc=tc: e.activation(
                    out=ST["lnV"][:, tc:tc + 1], in_=ST["sqs2"][:, tc:tc + 1],
                    func=AF.Ln, scale=1.0 / D, bias=epsT[:]), waits=[(s_dve, dv)],
                    inc=s_act)
                av = S.op("scalar", lambda e, tc=tc: e.activation(
                    out=ST["r2b"][:, tc:tc + 1], in_=ST["lnV"][:, tc:tc + 1],
                    func=AF.Exp, scale=-0.5), waits=[(s_act, av)], inc=s_act)
                av = S.op("scalar", lambda e, tc=tc: e.activation(
                    out=h2tile[:], in_=RES[:, tc * D:(tc + 1) * D], func=AF.Copy,
                    scale=ST["r2b"][:, tc:tc + 1]),
                    waits=[(s_act, av), (s_pe, h2_busy)], inc=s_act)
                for c in range(NDC):
                    ti = (tc * NDC + c) % 2
                    dv2 = transpose_to(
                        ps_trh[ti], h2tile[:, c * 128:(c + 1) * 128],
                        h2T[:, c * TBLK + tc * 128: c * TBLK + (tc + 1) * 128],
                        waits=[(s_act, av), (s_dve, trh_free[ti])],
                        scale=ffnw[:, c:c + 1])
                    trh_free[ti] = dv2
                h2_busy = S.val(s_pe)
            M.free("h2tile")
            M.free("ps_trh1", "ps_trh0", "ps_mm1", "ps_mm0")
            join()
            return RES, h2T
        RES, h2T = stage_s4()
        if out_stage == "z2":
            for tc in range(NTC):
                dma("sync", y_d[row0 + tc * 128: row0 + (tc + 1) * 128, :],
                    RES[:, tc * D:(tc + 1) * D], s_o[tc % 4],
                    waits=[(s_join, S.val(s_join))])
            for i in range(4):
                S.op("sync", lambda e, s=s_o[i], v=S.val(s_o[i]):
                     e.wait_ge(s, v))
            S.op("sync", lambda e: e.sem_inc(s_join, 4))
            S.bump(s_join, 4)
            M.free("h2T", "RES")
            continue
        S.paranoid = 's5' in paranoid
        def stage_s5():
            svT = M.sbuf("svT", [128, NIC * TBLK], BF16, side="left")
            sgt = [M.sbuf(f"sgt{i}", [128, 512], BF16, side="left") for i in range(2)]
            ws_up = WStream("up", wupT_d, D, 256, 4)
            ps_g = [M.psum(f"ps_g{i}", [128, 512], F32) for i in range(2)]
            ps_u = [M.psum(f"ps_u{i}", [128, 512], F32) for i in range(2)]
            g_free = [0, 0]
            u_free = [0, 0]
            for og in range(32):
                bi = ws_up.load(og * 256)
                wready = ws_up.bf_ready[bi]
                last_pe = None
                for ts in range(2):
                    pi = (og * 2 + ts) % 2
                    gpairs = [(ws_up.bf[bi][:, c * 256: c * 256 + 128],
                               h2T[:, c * TBLK + ts * 512:
                                   c * TBLK + (ts + 1) * 512])
                              for c in range(NDC)]
                    upairs = [(ws_up.bf[bi][:, c * 256 + 128: c * 256 + 256],
                               h2T[:, c * TBLK + ts * 512:
                                   c * TBLK + (ts + 1) * 512])
                              for c in range(NDC)]
                    pvg = mm_group(ps_g[pi][:], gpairs,
                                   extra_first_waits=[(s_act, wready),
                                                      (s_act, g_free[pi]),
                                                      (s_join, S.val(s_join))])
                    pvu = mm_group(ps_u[pi][:], upairs,
                                   extra_first_waits=[(s_dve, u_free[pi])])
                    last_pe = pvu
                    av = S.op("scalar", lambda e, pi=pi: e.activation(
                        out=sgt[pi][:], in_=ps_g[pi][:], func=AF.Silu),
                        waits=[(s_pe, pvg), (s_dve, u_free[pi])], inc=s_act)
                    g_free[pi] = av
                    dst = svT[:, og * TBLK + ts * 512: og * TBLK + (ts + 1) * 512]
                    dv = S.op("vector", lambda e, pi=pi, d=dst:
                              e.tensor_tensor(out=d, in0=ps_u[pi][:],
                                              in1=sgt[pi][:], op=ALU.mult),
                              waits=[(s_pe, pvu), (s_act, av)], inc=s_dve)
                    u_free[pi] = dv
                ws_up.consumed(bi, last_pe)
            ws_up.done()
            M.free("sgt1", "sgt0")
            M.free("ps_u1", "ps_u0", "ps_g1", "ps_g0")
            M.free("h2T")
            join()
            return svT
        svT = stage_s5()
        S.paranoid = 's6' in paranoid
        def stage_s6():
            ws_wd = WStream("wd", wdT_d, INTER, 256, 8)
            ps_mm = [M.psum(f"ps_mm{i}", [128, 256], F32) for i in range(2)]
            mm_free = [0, 0]
            for oh in range(8):
                bi = ws_wd.load(oh * 256)
                wready = ws_wd.bf_ready[bi]
                last_pe = None
                for tc in range(NTC):
                    pi = (oh * NTC + tc) % 2
                    pairs = [(svT[:, i * TBLK + tc * 128: i * TBLK + (tc + 1) * 128],
                              ws_wd.bf[bi][:, i * 256:(i + 1) * 256])
                             for i in range(NIC)]
                    pv = mm_group(ps_mm[pi][:], pairs,
                                  extra_first_waits=[(s_act, wready),
                                                     (s_dve, mm_free[pi]),
                                                     (s_join, S.val(s_join))])
                    last_pe = pv
                    xi = tc % 2
                    xv = dma("scalar", xsl[xi][:, 0:256],
                             x_d[row0 + tc * 128: row0 + (tc + 1) * 128,
                                 oh * 256:(oh + 1) * 256], s_x[xi],
                             waits=[(s_dve, xsl_free[xi])])
                    oi = (oh * NTC + tc) % 4
                    res_sl = RES[:, tc * D + oh * 256: tc * D + (oh + 1) * 256]
                    dv = S.op("vector", lambda e, pi=pi, oi=oi, r=res_sl:
                              e.tensor_add(outr[oi][:, 0:256], ps_mm[pi][:], r),
                              waits=[(s_pe, pv), (s_o[oi], outr_free[oi])],
                              inc=s_dve)
                    mm_free[pi] = dv
                    dv = S.op("vector", lambda e, oi=oi, xi=xi: e.tensor_add(
                        outr[oi][:, 0:256], outr[oi][:, 0:256], xsl[xi][:, 0:256]),
                        waits=[(s_x[xi], xv), (s_dve, dv)], inc=s_dve)
                    xsl_free[xi] = dv
                    av1 = S.op("scalar", lambda e, oi=oi, tc=tc, oh=oh: e.activation(
                        out=junk[:, 0:256], in_=outr[oi][:, 0:256], func=AF.Copy,
                        accum_out=s34[:, tc * 8 + oh: tc * 8 + oh + 1]),
                        waits=[(s_dve, dv), (s_act, S.val(s_act))], inc=s_act)
                    av2 = S.op("scalar", lambda e, oi=oi, tc=tc, oh=oh: e.activation(
                        out=junk[:, 0:256], in_=outr[oi][:, 0:256], func=AF.Square,
                        accum_out=s34[:, 64 + tc * 8 + oh: 64 + tc * 8 + oh + 1]),
                        waits=[(s_act, av1)], inc=s_act)
                    sv = dma("sync", spill_d[row0 + tc * 128: row0 + (tc + 1) * 128,
                                             oh * 256:(oh + 1) * 256],
                             outr[oi][:, 0:256], s_o[oi], waits=[(s_act, av2)])
                    outr_free[oi] = sv
                ws_wd.consumed(bi, last_pe)
            ws_wd.done()
            M.free("svT")
            M.free("ps_mm1", "ps_mm0")
            M.free("RES")
            join()
            return None
        stage_s6()
        if out_stage == "x1":
            for tc in range(NTC):
                for og in range(4):
                    xv = dma("sync", xsl[0][:],
                             spill_d[row0 + tc * 128: row0 + (tc + 1) * 128,
                                     og * 512:(og + 1) * 512], s_c,
                             waits=[(s_o[0], S.val(s_o[0]))])
                    dma("sync", y_d[row0 + tc * 128: row0 + (tc + 1) * 128,
                                    og * 512:(og + 1) * 512], xsl[0][:],
                        s_o[0], waits=[(s_c, xv)])
            continue
        S.paranoid = 's7' in paranoid
        def stage_s7():
            ln3T = M.sbuf("ln3T", [128, NDC * TBLK], BF16, side="right")
            l3tile = M.sbuf("l3tile", [128, D], BF16, side="left")
            xr = [M.sbuf(f"xr{i}", [128, D], F32, side="left") for i in range(2)]
            xr_free = [0, 0]
            ps_tr = [M.psum(f"ps_tr{i}", [128, 128], BF16) for i in range(2)]
            tr_free = [0, 0]
            l3_busy = 0
            spill_done = [(s_o[i], S.val(s_o[i])) for i in range(4)]
            for tc in range(NTC):
                dv = S.op("vector", lambda e, tc=tc: e.reduce_sum(
                    ST["sum3"][:, tc:tc + 1], s34[:, tc * 8:(tc + 1) * 8],
                    axis=AX.X),
                    waits=[(s_act, S.val(s_act)), (s_join, S.val(s_join))],
                    inc=s_dve)
                dv = S.op("vector", lambda e, tc=tc: e.reduce_sum(
                    ST["sq3"][:, tc:tc + 1], s34[:, 64 + tc * 8: 64 + (tc + 1) * 8],
                    axis=AX.X), waits=[(s_dve, dv)], inc=s_dve)
                dv = S.op("vector", lambda e, tc=tc: e.tensor_scalar(
                    out=ST["m3v"][:, tc:tc + 1], in0=ST["sum3"][:, tc:tc + 1],
                    scalar1=1.0 / D, scalar2=None, op0=ALU.mult),
                    waits=[(s_dve, dv)], inc=s_dve)
                dv = S.op("vector", lambda e, tc=tc: e.tensor_mul(
                    ST["tstat"][:, tc:tc + 1], ST["m3v"][:, tc:tc + 1],
                    ST["m3v"][:, tc:tc + 1]), waits=[(s_dve, dv)], inc=s_dve)
                dv = S.op("vector", lambda e, tc=tc: e.scalar_tensor_tensor(
                    out=ST["var3"][:, tc:tc + 1], in0=ST["sq3"][:, tc:tc + 1],
                    scalar=1.0 / D, in1=ST["tstat"][:, tc:tc + 1],
                    op0=ALU.mult, op1=ALU.subtract), waits=[(s_dve, dv)], inc=s_dve)
                av = S.op("scalar", lambda e, tc=tc: e.activation(
                    out=ST["ln3A"][:, tc:tc + 1], in_=ST["var3"][:, tc:tc + 1],
                    func=AF.Ln, bias=epsT[:]), waits=[(s_dve, dv)], inc=s_act)
                av = S.op("scalar", lambda e, tc=tc: e.activation(
                    out=ST["s3"][:, tc:tc + 1], in_=ST["ln3A"][:, tc:tc + 1],
                    func=AF.Exp, scale=-0.5), waits=[(s_act, av)], inc=s_act)
                dv = S.op("vector", lambda e, tc=tc: e.scalar_tensor_tensor(
                    out=ST["b3"][:, tc:tc + 1], in0=ST["m3v"][:, tc:tc + 1],
                    scalar=-1.0, in1=ST["s3"][:, tc:tc + 1], op0=ALU.mult,
                    op1=ALU.mult), waits=[(s_act, av)], inc=s_dve)
                ri = tc % 2
                xv = dma("scalar", xr[ri][:],
                         spill_d[row0 + tc * 128: row0 + (tc + 1) * 128, :],
                         s_x[ri],
                         waits=[(s_pool, xr_free[ri])] + spill_done)
                pl = S.op("gpsimd", lambda e, ri=ri, tc=tc: e.tensor_scalar(
                    out=l3tile[:], in0=xr[ri][:],
                    scalar1=ST["s3"][:, tc:tc + 1],
                    scalar2=ST["b3"][:, tc:tc + 1],
                    op0=ALU.mult, op1=ALU.add),
                    waits=[(s_x[ri], xv), (s_dve, dv), (s_pool, S.val(s_pool)),
                           (s_pe, l3_busy)],
                    inc=s_pool)
                xr_free[ri] = pl
                for c in range(NDC):
                    ti = (tc * NDC + c) % 2
                    dv2 = transpose_to(
                        ps_tr[ti], l3tile[:, c * 128:(c + 1) * 128],
                        ln3T[:, c * TBLK + tc * 128: c * TBLK + (tc + 1) * 128],
                        waits=[(s_pool, pl), (s_dve, tr_free[ti])])
                    tr_free[ti] = dv2
                l3_busy = S.val(s_pe)
            M.free("xr1", "xr0", "l3tile")
            M.free("ps_tr1", "ps_tr0")
            join()
            return ln3T
        ln3T = stage_s7()
        S.paranoid = 's8a' in paranoid
        def stage_s8a():
            comb = M.sbuf("comb", [128, NTC * NE], F32, side="left")
            c8 = {n: M.sbuf("c8_" + n, [128, NE], F32, side="left")
                  for n in ("ex", "sc", "tmp")}
            c8u = M.sbuf("c8u", [128, NE], U32, side="left")
            c1 = {n: M.sbuf("c1_" + n, [128, 1], F32, side="left")
                  for n in ("nm", "sm", "rc", "m1", "m2", "nb", "z1")}
            ps_lg = M.psum("ps_lg", [128, NE], F32)
            gwb_ready = S.val(s_pool)
            lg_free = 0
            S.op("vector", lambda e: e.memset(c1["nb"][:], NEG_BIG),
                 waits=[(s_join, S.val(s_join))], inc=s_dve)
            S.op("vector", lambda e: e.memset(c1["z1"][:], 0.0), inc=s_dve)
            for tc in range(NTC):
                pairs = [(ln3T[:, c * TBLK + tc * 128: c * TBLK + (tc + 1) * 128],
                          gwb[:, c * NE:(c + 1) * NE]) for c in range(NDC)]
                pv = mm_group(ps_lg[:], pairs,
                              extra_first_waits=[(s_pool, gwb_ready),
                                                 (s_act, lg_free),
                                                 (s_join, S.val(s_join))])
                dv = S.op("vector", lambda e: e.reduce_max(
                    c1["nm"][:], ps_lg[:], axis=AX.X, negate=True),
                    waits=[(s_pe, pv), (s_dve, S.val(s_dve))], inc=s_dve)
                av = S.op("scalar", lambda e: e.activation(
                    out=c8["ex"][:], in_=ps_lg[:], func=AF.Exp, bias=c1["nm"][:],
                    accum_out=c1["sm"][:]),
                    waits=[(s_dve, dv), (s_act, S.val(s_act))], inc=s_act)
                lg_free = av
                dv = S.op("vector", lambda e: e.reciprocal(c1["rc"][:], c1["sm"][:]),
                          waits=[(s_act, av)], inc=s_dve)
                av = S.op("scalar", lambda e: e.activation(
                    out=c8["sc"][:], in_=c8["ex"][:], func=AF.Copy,
                    scale=c1["rc"][:]), waits=[(s_dve, dv)], inc=s_act)
                dv = S.op("vector", lambda e: e.reduce_max(
                    c1["m1"][:], c8["sc"][:], axis=AX.X), waits=[(s_act, av)],
                    inc=s_dve)
                dv = S.op("vector", lambda e: e.tensor_scalar(
                    out=c8["tmp"][:], in0=c8["sc"][:], scalar1=c1["m1"][:],
                    scalar2=None, op0=ALU.subtract), waits=[(s_dve, dv)], inc=s_dve)
                dv = S.op("vector", lambda e: e.tensor_scalar(
                    out=c8u[:], in0=c8["tmp"][:], scalar1=0.0, scalar2=None,
                    op0=ALU.is_ge), waits=[(s_dve, dv)], inc=s_dve)
                dv = S.op("vector", lambda e: e.tensor_copy(
                    out=c8["tmp"][:], in_=c8["sc"][:]), waits=[(s_dve, dv)],
                    inc=s_dve)
                dv = S.op("vector", lambda e: e.copy_predicated(
                    c8["tmp"][:], c8u[:], c1["nb"][:].broadcast_to([128, NE])),
                    waits=[(s_dve, dv)], inc=s_dve)
                dv = S.op("vector", lambda e: e.reduce_max(
                    c1["m2"][:], c8["tmp"][:], axis=AX.X), waits=[(s_dve, dv)],
                    inc=s_dve)
                dv = S.op("vector", lambda e: e.tensor_scalar(
                    out=c8["tmp"][:], in0=c8["sc"][:], scalar1=c1["m2"][:],
                    scalar2=None, op0=ALU.subtract), waits=[(s_dve, dv)], inc=s_dve)
                dv = S.op("vector", lambda e: e.tensor_scalar(
                    out=c8u[:], in0=c8["tmp"][:], scalar1=0.0, scalar2=None,
                    op0=ALU.is_ge), waits=[(s_dve, dv)], inc=s_dve)
                dv = S.op("vector", lambda e, tc=tc: e.memset(
                    comb[:, tc * NE:(tc + 1) * NE], 0.0), waits=[(s_dve, dv)],
                    inc=s_dve)
                dv = S.op("vector", lambda e, tc=tc: e.copy_predicated(
                    comb[:, tc * NE:(tc + 1) * NE], c8u[:], c8["sc"][:]),
                    waits=[(s_dve, dv)], inc=s_dve)
            comb_done = S.val(s_dve)
            M.free("ps_lg")
            return comb, comb_done
        comb, comb_done = stage_s8a()
        S.paranoid = 's8b' in paranoid
        def stage_s8b():
            gbuf = M.sbuf("gbuf", [128, NHC * TBLK], BF16, side="left")
            racc = M.sbuf("racc", [128, NTC * D], F32, side="right")
            sgt = [M.sbuf(f"sgt{i}", [128, 512], BF16, side="left") for i in range(2)]
            ws_gu = WStream("gu", mgu_d, D, 256, 8)
            ws_dn = WStream("dn", mdown_d, HM, 256, 6)
            ps_g = [M.psum(f"ps_g{i}", [128, 512], F32) for i in range(2)]
            ps_u = [M.psum(f"ps_u{i}", [128, 512], F32) for i in range(2)]
            ps_dn = [M.psum(f"ps_dn{i}", [128, 256], F32) for i in range(2)]
            g_free = [0, 0]
            u_free = [0, 0]
            gb_free = 0
            dn_free = [0, 0]
            racc_free = 0

            for ex in range(NE):
                for og in range(NHC):
                    bi = ws_gu.load(og * 256, rc0=ex * NDC)
                    wready = ws_gu.bf_ready[bi]
                    last_pe = None
                    for ts in range(2):
                        pi = (og * 2 + ts) % 2
                        gpairs = [(ws_gu.bf[bi][:, c * 256: c * 256 + 128],
                                   ln3T[:, c * TBLK + ts * 512:
                                        c * TBLK + (ts + 1) * 512])
                                  for c in range(NDC)]
                        upairs = [(ws_gu.bf[bi][:, c * 256 + 128: c * 256 + 256],
                                   ln3T[:, c * TBLK + ts * 512:
                                        c * TBLK + (ts + 1) * 512])
                                  for c in range(NDC)]
                        pvg = mm_group(ps_g[pi][:], gpairs,
                                       extra_first_waits=[(s_act, wready),
                                                          (s_act, g_free[pi]),
                                                          (s_pe, gb_free),
                                                          (s_dve, comb_done),
                                                          (s_join, S.val(s_join))])
                        pvu = mm_group(ps_u[pi][:], upairs,
                                       extra_first_waits=[(s_dve, u_free[pi])])
                        last_pe = pvu
                        av = S.op("scalar", lambda e, pi=pi: e.activation(
                            out=sgt[pi][:], in_=ps_g[pi][:], func=AF.Silu),
                            waits=[(s_pe, pvg), (s_dve, u_free[pi])], inc=s_act)
                        g_free[pi] = av
                        dst = gbuf[:, og * TBLK + ts * 512:
                                   og * TBLK + (ts + 1) * 512]
                        dv = S.op("vector", lambda e, pi=pi, d=dst:
                                  e.tensor_tensor(out=d, in0=ps_u[pi][:],
                                                  in1=sgt[pi][:], op=ALU.mult),
                                  waits=[(s_pe, pvu), (s_act, av)], inc=s_dve)
                        u_free[pi] = dv
                    ws_gu.consumed(bi, last_pe)
                gu_done = S.val(s_dve)
                for oh in range(8):
                    bi = ws_dn.load(oh * 256, rc0=ex * NHC, nrc=NHC)
                    wready = ws_dn.bf_ready[bi]
                    last_pe = None
                    for tc in range(NTC):
                        pi = (oh * NTC + tc) % 2
                        pairs = [(gbuf[:, hcc * TBLK + tc * 128:
                                       hcc * TBLK + (tc + 1) * 128],
                                  ws_dn.bf[bi][:, hcc * 256:(hcc + 1) * 256])
                                 for hcc in range(NHC)]
                        pv = mm_group(ps_dn[pi][:], pairs,
                                      extra_first_waits=[(s_dve, gu_done),
                                                         (s_act, wready),
                                                         (s_dve, dn_free[pi])])
                        last_pe = pv
                        racc_sl = racc[:, tc * D + oh * 256:
                                       tc * D + (oh + 1) * 256]
                        csl = comb[:, tc * NE + ex: tc * NE + ex + 1]
                        if ex == 0:
                            dv = S.op("vector", lambda e, pi=pi, r=racc_sl, c=csl:
                                      e.tensor_scalar(out=r, in0=ps_dn[pi][:],
                                                      scalar1=c, scalar2=None,
                                                      op0=ALU.mult),
                                      waits=[(s_pe, pv), (s_dve, comb_done),
                                             (s_dve, racc_free)], inc=s_dve)
                        else:
                            dv = S.op("vector", lambda e, pi=pi, r=racc_sl, c=csl:
                                      e.scalar_tensor_tensor(
                                          out=r, in0=ps_dn[pi][:], scalar=c,
                                          in1=r, op0=ALU.mult, op1=ALU.add),
                                      waits=[(s_pe, pv)], inc=s_dve)
                        dn_free[pi] = dv
                    ws_dn.consumed(bi, last_pe)
                gb_free = S.val(s_pe)
            ws_dn.done()
            ws_gu.done()
            M.free("sgt1", "sgt0")
            M.free("gbuf")
            M.free("ps_dn1", "ps_dn0", "ps_u1", "ps_u0", "ps_g1", "ps_g0")
            racc_done = S.val(s_dve)
            join()
            return racc, racc_done, dn_free, gb_free
        racc, racc_done, dn_free, gb_free = stage_s8b()
        S.paranoid = 's9' in paranoid
        def stage_s9():
            svSh = M.sbuf("svSh", [128, NSC * TBLK], BF16, side="left")
            sgt = [M.sbuf(f"sgt{i}", [128, 512], BF16, side="left") for i in range(2)]
            ws_sgu = WStream("sgu", swgu_d, D, 256, 8)
            ps_g = [M.psum(f"ps_g{i}", [128, 512], F32) for i in range(2)]
            ps_u = [M.psum(f"ps_u{i}", [128, 512], F32) for i in range(2)]
            ps_dn = [M.psum(f"ps_dn{i}", [128, 256], F32) for i in range(2)]
            dn_free = [0, 0]
            g_free = [0, 0]
            u_free = [0, 0]
            for og in range(NSC):
                bi = ws_sgu.load(og * 256)
                wready = ws_sgu.bf_ready[bi]
                last_pe = None
                for ts in range(2):
                    pi = (og * 2 + ts) % 2
                    gpairs = [(ws_sgu.bf[bi][:, c * 256: c * 256 + 128],
                               ln3T[:, c * TBLK + ts * 512:
                                    c * TBLK + (ts + 1) * 512])
                              for c in range(NDC)]
                    upairs = [(ws_sgu.bf[bi][:, c * 256 + 128: c * 256 + 256],
                               ln3T[:, c * TBLK + ts * 512:
                                    c * TBLK + (ts + 1) * 512])
                              for c in range(NDC)]
                    pvg = mm_group(ps_g[pi][:], gpairs,
                                   extra_first_waits=[(s_act, wready),
                                                      (s_act, g_free[pi]),
                                                      (s_pe, gb_free)])
                    pvu = mm_group(ps_u[pi][:], upairs,
                                   extra_first_waits=[(s_dve, u_free[pi])])
                    last_pe = pvu
                    av = S.op("scalar", lambda e, pi=pi: e.activation(
                        out=sgt[pi][:], in_=ps_g[pi][:], func=AF.Silu),
                        waits=[(s_pe, pvg), (s_dve, u_free[pi])], inc=s_act)
                    g_free[pi] = av
                    dst = svSh[:, og * TBLK + ts * 512: og * TBLK + (ts + 1) * 512]
                    dv = S.op("vector", lambda e, pi=pi, d=dst: e.tensor_tensor(
                        out=d, in0=ps_u[pi][:], in1=sgt[pi][:], op=ALU.mult),
                        waits=[(s_pe, pvu), (s_act, av)], inc=s_dve)
                    u_free[pi] = dv
                ws_sgu.consumed(bi, last_pe)
            ws_sgu.done()
            svsh_done = S.val(s_dve)
            ws_sd = WStream("sd", swd_d, HS, 256, 11)
            for oh in range(8):
                bi = ws_sd.load(oh * 256)
                wready = ws_sd.bf_ready[bi]
                last_pe = None
                for tc in range(NTC):
                    pi = (oh * NTC + tc) % 2
                    pairs = [(svSh[:, sc_ * TBLK + tc * 128:
                                  sc_ * TBLK + (tc + 1) * 128],
                              ws_sd.bf[bi][:, sc_ * 256:(sc_ + 1) * 256])
                             for sc_ in range(NSC)]
                    pv = mm_group(ps_dn[pi][:], pairs,
                                  extra_first_waits=[(s_dve, svsh_done),
                                                     (s_act, wready),
                                                     (s_dve, dn_free[pi])])
                    last_pe = pv
                    xi = tc % 2
                    xv = dma("scalar", xsl[xi][:, 0:256],
                             spill_d[row0 + tc * 128: row0 + (tc + 1) * 128,
                                     oh * 256:(oh + 1) * 256], s_x[xi],
                             waits=[(s_dve, xsl_free[xi])])
                    oi = (oh * NTC + tc) % 4
                    racc_sl = racc[:, tc * D + oh * 256: tc * D + (oh + 1) * 256]
                    dv = S.op("vector", lambda e, pi=pi, oi=oi, r=racc_sl:
                              e.tensor_add(outr[oi][:, 0:256], ps_dn[pi][:], r),
                              waits=[(s_pe, pv), (s_dve, racc_done),
                                     (s_o[oi], outr_free[oi])], inc=s_dve)
                    dn_free[pi] = dv
                    dv = S.op("vector", lambda e, oi=oi, xi=xi: e.tensor_add(
                        outr[oi][:, 0:256], outr[oi][:, 0:256], xsl[xi][:, 0:256]),
                        waits=[(s_x[xi], xv), (s_dve, dv)], inc=s_dve)
                    xsl_free[xi] = dv
                    yv = dma("sync", y_d[row0 + tc * 128: row0 + (tc + 1) * 128,
                                         oh * 256:(oh + 1) * 256],
                             outr[oi][:, 0:256], s_o[oi], waits=[(s_dve, dv)])
                    outr_free[oi] = yv
                ws_sd.consumed(bi, last_pe)
            ws_sd.done()
            M.free("sgt1", "sgt0", "svSh")
            M.free("ps_dn1", "ps_dn0", "ps_u1", "ps_u0", "ps_g1", "ps_g0")
            M.free("c1_z1", "c1_nb", "c1_m2", "c1_m1", "c1_rc", "c1_sm", "c1_nm")
            M.free("c8u", "c8_tmp", "c8_sc", "c8_ex", "comb")
            M.free("racc", "ln3T")
            join()
            return None
        stage_s9()

    S.paranoid = False
    for i in range(4):
        S.op("sync", lambda e, i=i, v=S.val(s_o[i]): e.wait_ge(s_o[i], v))

    stats = S.emit(block)
    block_guard.__exit__(None, None, None)
    total = sum(stats.values())
    print(f"[kernel] instruction ops per engine: {stats} total={total}")
    return nc


def _host_consts():
    identb = np.eye(128, dtype=ml_dtypes.bfloat16)
    maskt = np.triu(np.full((128, 128), NEG_BIG, dtype=np.float32), k=1)
    onesf = np.ones((128, 128), dtype=np.float32)
    return identb, maskt, onesf


def _rope_tables(core):
    flat = np.arange(core * T_CORE, (core + 1) * T_CORE, dtype=np.int64)
    s = flat % 4096
    inv = 1.0 / (10000.0 ** (np.arange(0, DH, 2, dtype=np.float64) / DH))
    ang = np.outer(s.astype(np.float64), inv)
    cosc = np.concatenate([np.cos(ang), np.cos(ang)], axis=1).astype(np.float32)
    sintc = np.concatenate([-np.sin(ang), np.sin(ang)], axis=1).astype(np.float32)
    return cosc, sintc


def _interleave_pairs(a, b):
    D_, N = a.shape
    nch = N // 128
    out = np.empty((D_, 2 * N), dtype=a.dtype)
    av = a.reshape(D_, nch, 128)
    bv = b.reshape(D_, nch, 128)
    o = out.reshape(D_, nch, 2, 128)
    o[:, :, 0, :] = av
    o[:, :, 1, :] = bv
    return out


def kernel(**inputs):
    return _kernel_impl(inputs, OUT_STAGE)


def _kernel_impl(inputs, out_stage, paranoid=()):
    x = np.ascontiguousarray(
        np.asarray(inputs["L0_attn_x"], dtype=np.float32).reshape(-1, D))
    f32 = lambda k: np.ascontiguousarray(np.asarray(inputs[k], dtype=np.float32))

    wqkvT = np.ascontiguousarray(f32("L0_attn_w_qkv").T)
    woT = np.ascontiguousarray(f32("L0_attn_w_o").T)
    wup = f32("L0_attn_w_up")
    wupT_g = np.ascontiguousarray(wup[:INTER].T)
    wupT_v = np.ascontiguousarray(wup[INTER:].T)
    wupT = _interleave_pairs(wupT_g, wupT_v)
    wdT = np.ascontiguousarray(f32("L0_attn_w_down").T)
    gatew = f32("L0_ffn_gate_w")
    mg = f32("L0_ffn_w_gate")
    mu = f32("L0_ffn_w_up")
    mgu = np.stack([_interleave_pairs(mg[e], mu[e]) for e in range(NE)])
    mgu = np.ascontiguousarray(mgu.reshape(NE * D, 2 * HM))
    mdown = np.ascontiguousarray(f32("L0_ffn_w_down").reshape(NE * HM, D))
    swgu = _interleave_pairs(f32("L0_ffn_sw_gate"), f32("L0_ffn_sw_up"))
    swd = f32("L0_ffn_sw_down")
    attnw_in = np.ascontiguousarray(
        f32("L0_attn_attn_norm_w").reshape(NDC, 128).T)
    ffnw_in = np.ascontiguousarray(f32("L0_attn_ffn_norm_w").reshape(NDC, 128).T)

    identb, maskt, onesf = _host_consts()

    nc = build_program(out_stage, paranoid)
    in_maps = []
    for core in range(NCORE):
        cosc, sintc = _rope_tables(core)
        in_maps.append(dict(
            x=np.ascontiguousarray(x[core * T_CORE:(core + 1) * T_CORE]),
            wqkvT=wqkvT, woT=woT, wupT=wupT, wdT=wdT, gatew=gatew,
            mgu=mgu, mdown=mdown, swgu=swgu, swd=swd,
            cosc=cosc, sintc=sintc, identb=identb, maskt=maskt,
            onesf=onesf, attnw_in=attnw_in, ffnw_in=ffnw_in,
        ))
    res = run_bass_kernel_spmd(nc, in_maps, core_ids=list(range(NCORE)))
    y = np.concatenate([r["y"] for r in res.results], axis=0)
    return y.reshape(4, 4096, D)
